# version 01 (final)
# baseline (speedup 1.0000x reference)
import numpy as np
import ml_dtypes
from contextlib import ExitStack

import concourse.bass as bass
import concourse.bass_types as bass_types
import concourse.mybir as mybir
from concourse.bass_utils import run_bass_kernel_spmd


def bcast_mid(ap, n):
    return bass_types.AP(tensor=ap.tensor, offset=ap.offset,
                         ap=[list(ap.ap[0]), [0, n], list(ap.ap[1])])

F32 = mybir.dt.float32
BF16 = mybir.dt.bfloat16
AF = mybir.ActivationFunctionType
ALU = mybir.AluOpType
AX = mybir.AxisListType

T = 2048
H = 2048
NH, HD = 16, 128
NFW, FD, CHUNK = 4, 512, 512
I = 8192
NT = T // 128
NF = H // 128
NS = T // 512
NC = T // CHUNK
BASE_LR_INV = float(np.log(np.expm1(0.001)))
ROPE_BASE = 500000.0

ENGS = ("sync", "vector", "scalar", "tensor", "gpsimd")
ALLSEMS = ("ld", "st", "t", "v", "s", "g", "dt")

(R_ONEPCA, R_SA, R_GA, R_ONEPCB, R_SB, R_GB,
 R_U0, R_O0, R_U1, R_O1, R_QNW, R_KNW) = range(12)
NROWS = 12
(C_U0, C_O0, C_U1, C_O1, C_QNW, C_KNW, C_GA, C_GB) = range(8)
NCOLS = 8


class B:
    def __init__(self):
        self.nc = bass.Bass("TRN2", target_bir_lowering=False,
                            detect_race_conditions=False)
        self.ctx = ExitStack()
        self.q = {e: [] for e in ENGS}
        self.semh = {}
        self.cnt = {}
        self.pending = {e: [] for e in ENGS}

    def sem(self, name):
        self.semh[name] = self.ctx.enter_context(self.nc.semaphore(name))
        self.cnt[name] = 0

    def sbuf(self, name, shape, dtype=F32):
        return self.ctx.enter_context(self.nc.sbuf_tensor(name, shape, dtype))

    def psum(self, name, shape, dtype=F32):
        return self.ctx.enter_context(self.nc.psum_tensor(name, shape, dtype))

    def dram(self, name, shape, dtype=F32, kind="Internal"):
        return self.nc.dram_tensor(name, shape, dtype, kind=kind).ap()

    def _mk(self, eng, fn, wait, inc, amt):
        waits = list(self.pending[eng]) + list(wait or ())
        self.pending[eng] = []
        incs = []
        if inc:
            self.cnt[inc] += amt
            incs.append((inc, amt))
        semh = self.semh

        def run(e):
            seen = {}
            for s, v in waits:
                if v > 0 and seen.get(s, -1) < v:
                    seen[s] = v
                    e.wait_ge(semh[s], v)
            ins = fn(e)
            for s, n in incs:
                ins.then_inc(semh[s], n)
        self.q[eng].append(run)

    def cop(self, eng, fn, wait=None, inc=None):
        self._mk(eng, fn, wait, inc, 1)

    def dma(self, eng, fn, wait=None, inc=None):
        self._mk(eng, fn, wait, inc, 16)

    def sdma(self, eng, fn, sem, wait=None):
        w = list(wait or ())
        if self.cnt[sem] >= 16:
            w.append((sem, self.cnt[sem]))
        self._mk(eng, fn, w, sem, 16)

    def mark(self, sem):
        return (sem, self.cnt[sem])

    def barrier(self):
        marks = [(s, self.cnt[s]) for s in self.cnt]
        for e in ENGS:
            self.pending[e] = list(marks)

    def finish(self):
        nc = self.nc
        with self.ctx:
            with nc.Block() as block:
                @block.sync
                def _(e):
                    for f in self.q["sync"]:
                        f(e)

                @block.vector
                def _(e):
                    for f in self.q["vector"]:
                        f(e)

                @block.scalar
                def _(e):
                    for f in self.q["scalar"]:
                        f(e)

                @block.tensor
                def _(e):
                    for f in self.q["tensor"]:
                        f(e)

                @block.gpsimd
                def _(e):
                    for f in self.q["gpsimd"]:
                        f(e)
        return nc


def build(dbg_names=(), lvl=99):
    b = B()
    for s in ALLSEMS:
        b.sem(s)
    for s in ("wm0", "wm1", "wm2", "wt0", "wt1", "h10", "h11", "x0", "x1",
              "sh0", "sh1", "sr", "rb", "so0", "so1", "so2", "so3"):
        b.sem(s)

    x_in = b.dram("x", [T, H], F32, kind="ExternalInput")
    c_in = b.dram("cvec", [1, H], F32, kind="ExternalInput")
    wmod_in = b.dram("wmod", [H, 6 * H], F32, kind="ExternalInput")
    qkv_w = b.dram("qkv_w", [3 * H, H], F32, kind="ExternalInput")
    qnw_in = b.dram("q_norm_w", [1, H], F32, kind="ExternalInput")
    knw_in = b.dram("k_norm_w", [1, H], F32, kind="ExternalInput")
    qks_in = b.dram("qk_scale", [1, 2 * H], F32, kind="ExternalInput")
    qko_in = b.dram("qk_offset", [1, 2 * H], F32, kind="ExternalInput")
    lrw_in = b.dram("lr_w", [12, H], F32, kind="ExternalInput")
    lrb_in = b.dram("lr_b", [1, 12], F32, kind="ExternalInput")
    w0_in = b.dram("w0", [NFW * FD, FD], F32, kind="ExternalInput")
    w1_in = b.dram("w1", [NFW * FD, FD], F32, kind="ExternalInput")
    w2_in = b.dram("w2", [NFW * FD, FD], F32, kind="ExternalInput")
    tttnw_in = b.dram("ttt_norm_w", [1, FD], F32, kind="ExternalInput")
    wo_in = b.dram("o_proj_w", [H, H], F32, kind="ExternalInput")
    wg_in = b.dram("w_gate", [I, H], F32, kind="ExternalInput")
    wu_in = b.dram("w_up", [I, H], F32, kind="ExternalInput")
    wd_in = b.dram("w_down", [H, I], F32, kind="ExternalInput")
    out = b.dram("out", [T, H], F32, kind="ExternalOutput")

    rowbank = b.dram("rowbank", [NROWS, H])
    h_scr = b.dram("h_scr", [T, H], BF16)
    qkv16_scr = b.dram("qkv16_scr", [3 * H, H], BF16)
    fq_scr = b.dram("fq_scr", [H, T], BF16)
    fk_scr = b.dram("fk_scr", [H, T], BF16)
    fv_scr = b.dram("fv_scr", [H, T], BF16)
    qn_scr = b.dram("qn_scr", [H, T], BF16)
    kn_scr = b.dram("kn_scr", [H, T], BF16)
    v_scr = b.dram("v_scr", [H, T], BF16)
    qa_scr = b.dram("qa_scr", [T, H], BF16)
    ka_scr = b.dram("ka_scr", [T, H], BF16)
    attn_scr = b.dram("attn_scr", [H, T], BF16)
    ottt_scr = b.dram("ottt_scr", [T, H], BF16)
    wo16_scr = b.dram("wo16_scr", [H, H], BF16)
    branch_scr = b.dram("branch_scr", [H, T], BF16)
    x2_scr = b.dram("x2_scr", [T, H], F32)
    xq_scr = b.dram("xq_scr", [T, H], BF16)
    wqg_scr = b.dram("wqg_scr", [I, H], BF16)
    wqu_scr = b.dram("wqu_scr", [I, H], BF16)
    wqd_scr = b.dram("wqd_scr", [H, I], BF16)
    gu_scr = b.dram("gu_scr", [T, I], F32)
    guq_scr = b.dram("guq_scr", [T, I], BF16)
    branch2_scr = b.dram("b2_scr", [H, T], BF16)
    dbg_outs = {}

    F = b.sbuf("F", [128, 22528], F32)
    G = b.sbuf("G", [128, 45056], BF16)
    bpool = [b.sbuf(f"bc{i}", [128, H], BF16) for i in range(3)]
    colbank = b.sbuf("colbank", [128, NCOLS * 16], F32)
    l_slab = b.sbuf("l_slab", [128, NT * 12], F32)
    lrwT16 = b.sbuf("lrwT16", [128, NF * 12], BF16)
    lrb_col = b.sbuf("lrb_col", [12, 1], F32)
    ones1 = b.sbuf("ones1", [1, 128], BF16)
    ones1f = b.sbuf("ones1f", [1, 128], F32)
    onesc = b.sbuf("onesc", [128, 1], BF16)
    onescf = b.sbuf("onescf", [128, 1], F32)
    qslab = b.sbuf("qslab", [128, 96], F32)
    ident = b.sbuf("ident", [128, 128], BF16)
    identf32 = b.sbuf("identf32", [128, 128], F32)
    stats = b.sbuf("stats", [128, 128], F32)
    srow = b.sbuf("srow", [1, H], F32)
    srow16 = b.sbuf("srow16", [1, H], BF16)
    PS = b.psum("PS", [128, 3584], F32)
    pb = [PS[:, i * 512:(i + 1) * 512] for i in range(7)]
    pbt = b.psum("pbt", [128, 1024], BF16)

    def p0_tables():
        b.cop("gpsimd", lambda e: e.memset(ones1[:], 1.0))
        b.cop("gpsimd", lambda e: e.memset(ones1f[:], 1.0))
        b.cop("gpsimd", lambda e: e.memset(onesc[:], 1.0))
        b.cop("gpsimd", lambda e: e.memset(onescf[:], 1.0))
        b.cop("gpsimd", lambda e: e.memset(qslab[:], 0.0))
        b.cop("gpsimd", lambda e: e.iota(F[:, 0:128], [[1, 128]], channel_multiplier=-1,
                                         allow_small_or_imprecise_dtypes=True), inc="g")
        b.cop("vector", lambda e: e.tensor_scalar(ident[:], F[:, 0:128], 0.0, None, ALU.is_equal),
              wait=[("g", b.cnt["g"])], inc="v")
        b.cop("vector", lambda e: e.tensor_scalar(identf32[:], F[:, 0:128], 0.0, None, ALU.is_equal),
              inc="v")

    def p0_mod():
        mrow = F[0:1, 0:12288]
        csil = F[0:1, 12288:14336]
        crow = F[0:1, 20480:22528]
        scol32 = F[:, 20480:20496]
        wst32 = F[:, 12288:20480]
        csil16 = G[0:1, 0:H]
        scol16 = G[:, 18432:18448]
        wst16 = [G[:, 2048 + j * 8192:2048 + (j + 1) * 8192] for j in range(2)]
        b.sdma("sync", lambda e: e.dma_start(out=crow, in_=c_in[:, :]), "ld")
        b.cop("scalar", lambda e: e.activation(csil, crow, AF.Silu),
              wait=[("ld", b.cnt["ld"])], inc="s")
        b.cop("vector", lambda e: e.tensor_copy(csil16, csil),
              wait=[("s", b.cnt["s"])], inc="v")
        sv = b.cnt["s"]
        vv0 = b.cnt["v"]
        for hc in range(NF):
            b.cop("tensor", lambda e, hc=hc: e.matmul(
                pb[6][:, hc:hc + 1], csil16[:, hc * 128:(hc + 1) * 128], ones1[:, 0:1],
                start=True, stop=True),
                wait=[("v", vv0)] if hc == 0 else None)
        for hc in range(NF):
            b.cop("tensor", lambda e, hc=hc: e.matmul(
                pb[5][:, hc:hc + 1], csil[:, hc * 128:(hc + 1) * 128], ones1f[:, 0:1],
                start=True, stop=True),
                wait=[("s", sv)] if hc == 0 else None,
                inc="t" if hc == NF - 1 else None)
        b.cop("vector", lambda e: e.tensor_copy(scol16, pb[6][:, 0:16]),
              wait=[("t", b.cnt["t"])])
        b.cop("vector", lambda e: e.tensor_copy(scol32, pb[5][:, 0:16]), inc="v")
        scol_ready = ("v", b.cnt["v"])
        fp32_slices = set(range(8, 12)) | set(range(20, 24))
        evacs = []
        for n in range(24):
            f32p = n in fp32_slices
            if f32p:
                wsem = "wm2"
                b.sdma("gpsimd", lambda e, n=n: e.dma_start(
                    out=wst32[:, 0:8192].rearrange("p (c n) -> p c n", c=16),
                    in_=wmod_in[:, n * 512:(n + 1) * 512].rearrange("(c p) n -> p c n", p=128)),
                    wsem, wait=[("t", evacs[-1][0])] if evacs else None)
                buf, lhs = wst32[:, 0:8192], scol32
            else:
                bufj = n % 2
                wsem = "wm%d" % bufj
                b.sdma("gpsimd", lambda e, n=n, bufj=bufj: e.dma_start(
                    out=wst16[bufj].rearrange("p (c n) -> p c n", c=16),
                    in_=wmod_in[:, n * 512:(n + 1) * 512].rearrange("(c p) n -> p c n", p=128)),
                    wsem, wait=[("t", evacs[-2][0])] if len(evacs) >= 2 else None)
                buf, lhs = wst16[bufj], scol16
            mmw = [scol_ready, b.mark(wsem)]
            if len(evacs) >= 2:
                mmw.append(("v", evacs[-2][1]))
            bank = pb[n % 2]
            for hc in range(NF):
                b.cop("tensor", lambda e, hc=hc, buf=buf, bank=bank, lhs=lhs: e.matmul(
                    bank[0:1, :], lhs[:, hc:hc + 1], buf[:, hc * 512:(hc + 1) * 512],
                    start=(hc == 0), stop=(hc == NF - 1)),
                    wait=mmw if hc == 0 else None,
                    inc="t" if hc == NF - 1 else None)
            b.cop("vector", lambda e, n=n, bank=bank: e.tensor_copy(
                mrow[:, n * 512:(n + 1) * 512], bank[0:1, :]),
                wait=[("t", b.cnt["t"])], inc="v")
            evacs.append((b.cnt["t"], b.cnt["v"]))

        def sec(k):
            return mrow[:, k * H:(k + 1) * H]
        vv = [("v", b.cnt["v"])]
        b.sdma("sync", lambda e: e.dma_start(out=rowbank[R_SA:R_SA + 1, :], in_=sec(0)),
               "rb", wait=vv)
        b.sdma("sync", lambda e: e.dma_start(out=rowbank[R_SB:R_SB + 1, :], in_=sec(3)), "rb")
        b.cop("vector", lambda e: e.tensor_scalar(sec(1), sec(1), 1.0, None, ALU.add), wait=vv, inc="v")
        b.cop("vector", lambda e: e.tensor_scalar(sec(4), sec(4), 1.0, None, ALU.add),
              wait=[("v", b.cnt["v"])], inc="v")
        b.sdma("sync", lambda e: e.dma_start(out=rowbank[R_ONEPCA:R_ONEPCA + 1, :], in_=sec(1)),
               "rb", wait=[("v", b.cnt["v"])])
        b.sdma("sync", lambda e: e.dma_start(out=rowbank[R_ONEPCB:R_ONEPCB + 1, :], in_=sec(4)), "rb")
        for rdst, k in ((R_GA, 2), (R_GB, 5)):
            sg = F[0:1, 12288:12288 + H]
            ab = F[0:1, 14336:14336 + H]
            sgw = [b.mark("rb")]
            b.cop("vector", lambda e, k=k, sg=sg: e.tensor_scalar(sg, sec(k), 0.0, None, ALU.is_ge),
                  wait=sgw, inc="v")
            b.cop("vector", lambda e, sg=sg: e.tensor_scalar(sg, sg, 2.0, -1.0, ALU.mult, ALU.add),
                  wait=[("v", b.cnt["v"])], inc="v")
            b.cop("vector", lambda e, k=k, ab=ab: e.scalar_tensor_tensor(ab, sec(k), -1.0, sec(k), ALU.mult, ALU.max),
                  wait=[("v", b.cnt["v"])], inc="v")
            b.cop("vector", lambda e, ab=ab: e.tensor_scalar(ab, ab, 0.25, None, ALU.max),
                  wait=[("v", b.cnt["v"])], inc="v")
            b.cop("vector", lambda e, sg=sg, ab=ab: e.tensor_mul(sg, sg, ab),
                  wait=[("v", b.cnt["v"])], inc="v")
            b.sdma("sync", lambda e, r=rdst, sg=sg: e.dma_start(out=rowbank[r:r + 1, :], in_=sg),
                   "rb", wait=[("v", b.cnt["v"])])

    def p0_qk_rows():
        pr = F[0:1, 0:2 * H]
        orr = F[0:1, 2 * H:4 * H]
        qn = F[0:1, 4 * H:5 * H]
        kn = F[0:1, 5 * H:6 * H]
        b.sdma("sync", lambda e: e.dma_start(out=pr, in_=qks_in[:, :]),
               "ld", wait=[b.mark("rb")])
        b.sdma("sync", lambda e: e.dma_start(out=orr, in_=qko_in[:, :]), "ld")
        b.sdma("sync", lambda e: e.dma_start(out=qn, in_=qnw_in[:, :]), "ld")
        b.sdma("sync", lambda e: e.dma_start(out=kn, in_=knw_in[:, :]), "ld")
        lv = [("ld", b.cnt["ld"])]
        u0 = F[0:1, 6 * H:7 * H]
        u1 = F[0:1, 7 * H:8 * H]
        o0c = F[0:1, 8 * H:9 * H]
        o1c = F[0:1, 9 * H:10 * H]
        pr3 = pr.rearrange("o (h c) -> o h c", c=2)
        or3 = orr.rearrange("o (h c) -> o h c", c=2)
        b.cop("vector", lambda e: e.tensor_mul(u0, qn, pr3[:, :, 0]), wait=lv, inc="v")
        b.cop("vector", lambda e: e.tensor_mul(u1, kn, pr3[:, :, 1]), inc="v")
        b.cop("vector", lambda e: e.tensor_copy(o0c, or3[:, :, 0]), inc="v")
        b.cop("vector", lambda e: e.tensor_copy(o1c, or3[:, :, 1]), inc="v")
        vv = [("v", b.cnt["v"])]
        b.sdma("sync", lambda e: e.dma_start(out=rowbank[R_U0:R_U0 + 1, :], in_=u0), "rb", wait=vv)
        b.sdma("sync", lambda e: e.dma_start(out=rowbank[R_U1:R_U1 + 1, :], in_=u1), "rb")
        b.sdma("sync", lambda e: e.dma_start(out=rowbank[R_O0:R_O0 + 1, :], in_=o0c), "rb")
        b.sdma("sync", lambda e: e.dma_start(out=rowbank[R_O1:R_O1 + 1, :], in_=o1c), "rb")
        b.sdma("sync", lambda e: e.dma_start(out=rowbank[R_QNW:R_QNW + 1, :], in_=qn), "rb")
        b.sdma("sync", lambda e: e.dma_start(out=rowbank[R_KNW:R_KNW + 1, :], in_=kn), "rb")

    def p0_cols():
        for ci, r in ((C_U0, R_U0), (C_O0, R_O0), (C_U1, R_U1), (C_O1, R_O1),
                      (C_QNW, R_QNW), (C_KNW, R_KNW), (C_GA, R_GA), (C_GB, R_GB)):
            b.sdma("sync", lambda e, r=r: e.dma_start(out=srow[:], in_=rowbank[r:r + 1, :]),
                   "sr", wait=[b.mark("rb"), ("v", b.cnt["v"])])
            b.cop("vector", lambda e: e.tensor_copy(srow16[:], srow[:]),
                  wait=[b.mark("sr")], inc="v")
            sv = b.cnt["v"]
            for hc in range(NF):
                b.cop("tensor", lambda e, hc=hc: e.matmul(
                    pb[6][:, hc:hc + 1], srow16[:, hc * 128:(hc + 1) * 128], ones1[:, 0:1],
                    start=True, stop=True),
                    wait=[("v", sv)] if hc == 0 else None,
                    inc="t" if hc == NF - 1 else None)
            b.cop("vector", lambda e, ci=ci: e.tensor_copy(
                colbank[:, ci * 16:(ci + 1) * 16], pb[6][:, 0:16]),
                wait=[("t", b.cnt["t"])], inc="v")
        lwf = F[0:12, 0:H]
        lw = G[0:12, 0:H]
        b.sdma("sync", lambda e: e.dma_start(out=lwf, in_=lrw_in[0:12, :]),
               "ld", wait=[("v", b.cnt["v"])])
        b.cop("vector", lambda e: e.tensor_copy(lw, lwf), wait=[b.mark("ld")], inc="v")
        sv = b.cnt["v"]
        for hc in range(NF):
            b.cop("tensor", lambda e, hc=hc: e.transpose(
                pbt[:, hc * 12:(hc + 1) * 12], lw[:, hc * 128:(hc + 1) * 128], ident[0:12, 0:12]),
                wait=[("v", sv)] if hc == 0 else None,
                inc="t" if hc == NF - 1 else None)
        b.cop("vector", lambda e: e.tensor_copy(lrwT16[:], pbt[:, 0:NF * 12]),
              wait=[("t", b.cnt["t"])], inc="v")
        lrbr = F[0:1, 0:12]
        lrb16 = G[0:1, 2048:2060]
        b.sdma("sync", lambda e: e.dma_start(out=lrbr, in_=lrb_in[:, :]),
               "ld", wait=[("v", b.cnt["v"])])
        b.cop("vector", lambda e: e.tensor_copy(lrb16, lrbr), wait=[b.mark("ld")], inc="v")
        b.cop("tensor", lambda e: e.matmul(pb[6][0:12, 0:1], lrb16[:, 0:12], ones1[:, 0:1],
                                           start=True, stop=True),
              wait=[("v", b.cnt["v"])], inc="t")
        b.cop("vector", lambda e: e.tensor_scalar(lrb_col[:], pb[6][0:12, 0:1],
                                                  1.0, BASE_LR_INV, ALU.mult, ALU.add),
              wait=[("t", b.cnt["t"])], inc="v")

    def p3_rope():
        ct = F[:, 0:1024]
        st_t = F[:, 1024:2048]
        ctq = F[:, 2048:3072]
        stq = F[:, 3072:4096]
        fv_ = F[:, 4096:5120]
        iv = F[:, 5120:6144]
        uu = F[:, 6144:7168]
        cc = F[:, 7168:8192]
        mk = F[:, 8192:9216]
        sc1 = F[:, 9216:10240]
        b.cop("gpsimd", lambda e: e.iota(fv_, [[128, 16], [0, 64]], channel_multiplier=1,
                                         allow_small_or_imprecise_dtypes=True))
        b.cop("gpsimd", lambda e: e.iota(iv, [[0, 16], [1, 64]], channel_multiplier=0,
                                         allow_small_or_imprecise_dtypes=True), inc="g")
        gv = b.cnt["g"]
        lnb = float(np.log(ROPE_BASE) / 64.0)
        b.cop("scalar", lambda e: e.activation(iv, iv, AF.Exp, scale=-lnb),
              wait=[("g", gv)], inc="s")
        b.cop("vector", lambda e: e.tensor_mul(fv_, fv_, iv), wait=[("s", b.cnt["s"]), ("g", gv)])
        b.cop("vector", lambda e: e.tensor_scalar(uu, fv_, float(1.0 / (2 * np.pi)), None, ALU.mult))
        b.cop("vector", lambda e: e.tensor_scalar(mk, uu, 255.5, None, ALU.is_ge))
        b.cop("vector", lambda e: e.tensor_scalar(cc, mk, 256.0, None, ALU.mult))
        for bit in (128, 64, 32, 16, 8, 4, 2, 1):
            b.cop("vector", lambda e, bb=float(bit): e.scalar_tensor_tensor(
                mk, cc, bb - 0.5, uu, ALU.add, ALU.is_le))
            b.cop("vector", lambda e, bb=float(bit): e.scalar_tensor_tensor(
                cc, mk, bb, cc, ALU.mult, ALU.add))
        b.cop("vector", lambda e: e.scalar_tensor_tensor(
            fv_, cc, float(-2 * np.pi), fv_, ALU.mult, ALU.add), inc="v")
        b.cop("scalar", lambda e: e.activation(st_t, fv_, AF.Sin), wait=[("v", b.cnt["v"])])
        b.cop("scalar", lambda e: e.activation(uu, fv_, AF.Abs), inc="s")
        b.cop("vector", lambda e: e.tensor_scalar(uu, uu, -1.0, float(np.pi / 2), ALU.mult, ALU.add),
              wait=[("s", b.cnt["s"])], inc="v")
        b.cop("scalar", lambda e: e.activation(ct, uu, AF.Sin), wait=[("v", b.cnt["v"])], inc="s")
        s128 = float(1.0 / np.sqrt(HD))
        b.cop("vector", lambda e: e.tensor_scalar(ctq, ct, s128, None, ALU.mult),
              wait=[("s", b.cnt["s"])])
        b.cop("vector", lambda e: e.tensor_scalar(stq, st_t, s128, None, ALU.mult), inc="v")
        tab_ready = b.cnt["v"]

        tmb = [G[:, j * 2048:(j + 1) * 2048] for j in range(2)]
        otb = [G[:, 4096 + j * 2048:4096 + (j + 1) * 2048] for j in range(2)]
        dma_marks = {0: None, 1: None}
        out_marks = {0: None, 1: None}
        for xk in range(2):
            src = qn_scr if xk == 0 else kn_scr
            dst = qa_scr if xk == 0 else ka_scr
            cte = ctq if xk == 0 else ct
            ste = stq if xk == 0 else st_t
            for i in range(NT):
                par = (xk * NT + i) % 2
                tm = tmb[par]
                ob = otb[par]
                w = [out_marks[par]] if out_marks[par] else None
                b.sdma("sync", lambda e, src=src, i=i, tm=tm: e.dma_start_transpose(
                    tm, src[:, i * 128:(i + 1) * 128]), "h1%d" % par, wait=w)
                lv2 = b.mark("h1%d" % par)
                tm3 = tm.rearrange("p (h d) -> p h d", h=16)
                ob3 = ob.rearrange("p (h d) -> p h d", h=16)
                ctb = bcast_mid(cte[:, i * 64:(i + 1) * 64], 16)
                stb = bcast_mid(ste[:, i * 64:(i + 1) * 64], 16)
                vw = [lv2, ("v", tab_ready)]
                if dma_marks[par]:
                    vw.append(dma_marks[par])
                b.cop("vector", lambda e, tm3=tm3, ob3=ob3, ctb=ctb: e.tensor_tensor(
                    ob3[:, :, 0:64], tm3[:, :, 0:64], ctb, ALU.mult), wait=vw)
                b.cop("vector", lambda e, tm3=tm3, stb=stb: e.tensor_tensor(
                    sc1.rearrange("p (h d) -> p h d", h=16), tm3[:, :, 64:128], stb, ALU.mult))
                b.cop("vector", lambda e, ob3=ob3: e.tensor_sub(
                    ob3[:, :, 0:64], ob3[:, :, 0:64], sc1.rearrange("p (h d) -> p h d", h=16)))
                b.cop("vector", lambda e, tm3=tm3, ob3=ob3, ctb=ctb: e.tensor_tensor(
                    ob3[:, :, 64:128], tm3[:, :, 64:128], ctb, ALU.mult))
                b.cop("vector", lambda e, tm3=tm3, stb=stb: e.tensor_tensor(
                    sc1.rearrange("p (h d) -> p h d", h=16), tm3[:, :, 0:64], stb, ALU.mult))
                b.cop("vector", lambda e, ob3=ob3: e.tensor_add(
                    ob3[:, :, 64:128], ob3[:, :, 64:128], sc1.rearrange("p (h d) -> p h d", h=16)), inc="v")
                b.sdma("sync", lambda e, dst=dst, i=i, ob=ob: e.dma_start(
                    out=dst[i * 128:(i + 1) * 128, :], in_=ob), "so%d" % par,
                    wait=[("v", b.cnt["v"])])
                dma_marks[par] = b.mark("so%d" % par)
                out_marks[par] = ("v", b.cnt["v"])



    def p4_ttt():
        import os
        w0T_f = F[:, 0:2048]
        w2T_f = F[:, 2048:4096]
        w1_f = F[:, 4096:6144]
        w1T_f = F[:, 6144:8192]
        g1f = F[:, 8192:10240]
        g2f = F[:, 10240:12288]
        dhf = F[:, 12288:14336]
        sigf = F[:, 14336:16384]
        t1f = F[:, 16384:18432]
        t3f = F[:, 18432:20480]
        natf = F[:, 20480:22528]
        w0T16 = G[:, 0:2048]
        w2T16 = G[:, 2048:4096]
        w1_16 = G[:, 4096:6144]
        w1T16 = G[:, 6144:8192]
        fqc = G[:, 8192:10240]
        fkc = G[:, 10240:12288]
        fvc = G[:, 12288:14336]
        kctm = G[:, 14336:16384]
        vctm = G[:, 16384:18432]
        hq16 = G[:, 18432:20480]
        hid16 = G[:, 20480:22528]
        lv16 = G[:, 22528:24576]
        u016 = G[:, 24576:26624]
        u216 = G[:, 26624:28672]
        aq_sb = F[:, 8192:10240]
        ocn16 = G[:, 28672:29184]
        sqs = G[:, 29184:29696]

        b.sdma("sync", lambda e: e.dma_start(out=srow[0:1, 0:FD], in_=tttnw_in[:, :]), "sr",
               wait=[("v", b.cnt["v"])])
        b.cop("vector", lambda e: e.tensor_copy(srow16[0:1, 0:FD], srow[0:1, 0:FD]),
              wait=[b.mark("sr")], inc="v")
        b.cop("tensor", lambda e: e.matmul(pb[6], ones1[:], srow16[0:1, 0:FD],
                                           start=True, stop=True),
              wait=[("v", b.cnt["v"])], inc="t")
        b.cop("vector", lambda e: e.tensor_copy(bpool[2][:, 0:FD], pb[6]),
              wait=[("t", b.cnt["t"])], inc="v")
        tttw_b = bpool[2][:, 0:FD]

        for h in range(NFW):
            for (w_in, dstT) in ((w0_in, w0T_f), (w2_in, w2T_f)):
                b.sdma("sync", lambda e, w_in=w_in, h=h: e.dma_start(
                    out=natf.rearrange("p (c n) -> p c n", c=4),
                    in_=w_in[h * FD:(h + 1) * FD, :].rearrange("(c p) n -> p c n", p=128)),
                    "x0", wait=[("v", b.cnt["v"]), ("t", b.cnt["t"])])
                lm = b.mark("x0")
                for ec in range(4):
                    for dc in range(4):
                        bank = pb[(ec * 4 + dc) % 2]
                        w = [lm] if ec == 0 and dc == 0 else None
                        if ec * 4 + dc >= 2:
                            w = (w or []) + [("v", b.cnt["v"])]
                        b.cop("tensor", lambda e, ec=ec, dc=dc, bank=bank: e.transpose(
                            bank[:, 0:128], natf[:, ec * 512 + dc * 128:ec * 512 + (dc + 1) * 128],
                            identf32[:]), wait=w, inc="t")
                        b.cop("vector", lambda e, ec=ec, dc=dc, bank=bank, dstT=dstT: e.tensor_copy(
                            dstT[:, dc * 512 + ec * 128:dc * 512 + (ec + 1) * 128], bank[:, 0:128]),
                            wait=[("t", b.cnt["t"])], inc="v")
            b.sdma("sync", lambda e, h=h: e.dma_start(
                out=natf.rearrange("p (c n) -> p c n", c=4),
                in_=w1_in[h * FD:(h + 1) * FD, :].rearrange("(c p) n -> p c n", p=128)),
                "x0", wait=[("v", b.cnt["v"]), ("t", b.cnt["t"])])
            b.cop("vector", lambda e: e.tensor_copy(w1_f[:], natf[:]),
                  wait=[b.mark("x0")], inc="v")
            vv = b.cnt["v"]
            for ec in range(4):
                for dc in range(4):
                    bank = pb[(ec * 4 + dc) % 2]
                    w = [("v", vv)] if ec == 0 and dc == 0 else None
                    if ec * 4 + dc >= 2:
                        w = (w or []) + [("v", b.cnt["v"])]
                    b.cop("tensor", lambda e, ec=ec, dc=dc, bank=bank: e.transpose(
                        bank[:, 0:128], w1_f[:, dc * 512 + ec * 128:dc * 512 + (ec + 1) * 128],
                        identf32[:]), wait=w, inc="t")
                    b.cop("vector", lambda e, ec=ec, dc=dc, bank=bank: e.tensor_copy(
                        w1T_f[:, ec * 512 + dc * 128:ec * 512 + (dc + 1) * 128], bank[:, 0:128]),
                        wait=[("t", b.cnt["t"])], inc="v")
            for (mf, m16) in ((w0T_f, w0T16), (w2T_f, w2T16), (w1_f, w1_16), (w1T_f, w1T16)):
                b.cop("vector", lambda e, mf=mf, m16=m16: e.tensor_copy(m16[:], mf),
                      inc="v")
            st16_ready_v = b.cnt["v"]

            for c in range(int(os.environ.get('TTT_NC', NC))):
                lw = [("t", b.cnt["t"])]
                for (scr, dst, slot) in ((fq_scr, fqc, "h10"), (fk_scr, fkc, "h11"),
                                         (fv_scr, fvc, "wt1")):
                    b.sdma("sync", lambda e, scr=scr, dst=dst, h=h, c=c: e.dma_start(
                        out=dst.rearrange("p (a t) -> p a t", a=4),
                        in_=scr[h * FD:(h + 1) * FD, c * 512:(c + 1) * 512].rearrange(
                            "(a p) t -> p a t", p=128)), slot, wait=lw)
                    lw = None
                fql, fkl, fvl = b.mark("h10"), b.mark("h11"), b.mark("wt1")
                for pc in range(4):
                    b.sdma("sync", lambda e, h=h, c=c, pc=pc: e.dma_start_transpose(
                        kctm[:, pc * 512:(pc + 1) * 512],
                        fk_scr[h * FD:(h + 1) * FD, c * 512 + pc * 128:c * 512 + (pc + 1) * 128]),
                        "wt0", wait=[("t", b.cnt["t"])] if pc == 0 else None)
                    b.sdma("sync", lambda e, h=h, c=c, pc=pc: e.dma_start_transpose(
                        vctm[:, pc * 512:(pc + 1) * 512],
                        v_scr[0:0, 0:0] if False else fv_scr[h * FD:(h + 1) * FD,
                            c * 512 + pc * 128:c * 512 + (pc + 1) * 128]),
                        "wt0")
                tml = b.mark("wt0")

                sig_mark = b.cnt["s"]
                for ec in range(4):
                    for (srcW, bset) in ((w0T16, 0), (w2T16, 2)):
                        bank = pb[bset + ec % 2]
                        for dc in range(4):
                            w = None
                            if dc == 0:
                                w = [fql, ("v", st16_ready_v), ("s", sig_mark), ("v", b.cnt["v"])]
                            b.cop("tensor", lambda e, ec=ec, dc=dc, srcW=srcW, bank=bank: e.matmul(
                                bank, srcW[:, dc * 512 + ec * 128:dc * 512 + (ec + 1) * 128],
                                fqc[:, dc * 512:(dc + 1) * 512],
                                start=(dc == 0), stop=(dc == 3)),
                                wait=w, inc="t" if dc == 3 else None)
                    tmk = b.cnt["t"]
                    b.cop("scalar", lambda e, ec=ec: e.activation(
                        sigf[:, ec * 512:(ec + 1) * 512], pb[0 + ec % 2], AF.Sigmoid),
                        wait=[("t", tmk)], inc="s")
                    sig_mark = b.cnt["s"]
                    b.cop("vector", lambda e, ec=ec: e.tensor_copy(
                        aq_sb[:, ec * 512:(ec + 1) * 512], pb[0 + ec % 2]),
                        wait=[("s", sig_mark)], inc="v")
                    b.cop("vector", lambda e, ec=ec: e.tensor_mul(
                        aq_sb[:, ec * 512:(ec + 1) * 512], aq_sb[:, ec * 512:(ec + 1) * 512],
                        sigf[:, ec * 512:(ec + 1) * 512]),
                        wait=[("v", b.cnt["v"])], inc="v")
                    b.cop("vector", lambda e, ec=ec: e.tensor_tensor(
                        hq16[:, ec * 512:(ec + 1) * 512], aq_sb[:, ec * 512:(ec + 1) * 512],
                        pb[2 + ec % 2], ALU.mult),
                        wait=[("v", b.cnt["v"])], inc="v")
                hq_ready_v = b.cnt["v"]
                hq_ready_s = b.cnt["s"]

                for (lhs, rhsW, dstF, lmk) in ((fkc, w0T16, g1f, fkl), (fkc, w2T16, g2f, fkl),
                                               (fvc, w1_16, dhf, fvl)):
                    for ps in range(4):
                        bank = pb[4 + ps % 2]
                        for dc in range(4):
                            w = None
                            if dc == 0:
                                w = [lmk, ("v", b.cnt["v"]), ("s", hq_ready_s)]
                            b.cop("tensor", lambda e, ps=ps, dc=dc, lhs=lhs, rhsW=rhsW, bank=bank: e.matmul(
                                bank, lhs[:, dc * 512 + ps * 128:dc * 512 + (ps + 1) * 128],
                                rhsW[:, dc * 512:(dc + 1) * 512],
                                start=(dc == 0), stop=(dc == 3)),
                                wait=w, inc="t" if dc == 3 else None)
                        b.cop("vector", lambda e, ps=ps, dstF=dstF, bank=bank: e.tensor_copy(
                            dstF[:, ps * 512:(ps + 1) * 512], bank),
                            wait=[("t", b.cnt["t"])], inc="v")
                gv = b.cnt["v"]

                b.cop("scalar", lambda e: e.activation(sigf[:], g1f, AF.Sigmoid),
                      wait=[("v", gv)], inc="s")
                b.cop("vector", lambda e: e.tensor_mul(t1f, g1f, sigf),
                      wait=[("s", b.cnt["s"])], inc="v")
                b.cop("vector", lambda e: e.tensor_tensor(hid16[:], t1f, g2f, ALU.mult),
                      wait=[("v", b.cnt["v"])], inc="v")
                b.cop("vector", lambda e: e.scalar_tensor_tensor(t3f, t1f, -1.0, g1f, ALU.mult, ALU.add),
                      wait=[("v", b.cnt["v"])], inc="v")
                b.cop("vector", lambda e: e.tensor_scalar(t3f, t3f, 1.0, None, ALU.add),
                      wait=[("v", b.cnt["v"])], inc="v")
                b.cop("vector", lambda e: e.tensor_mul(t3f, t3f, sigf),
                      wait=[("v", b.cnt["v"])], inc="v")
                b.cop("vector", lambda e: e.tensor_mul(t1f, dhf, t1f),
                      wait=[("v", b.cnt["v"])], inc="v")
                for ps in range(4):
                    lcol2 = l_slab[:, (4 * c + ps) * 12 + 2 * 4 + h:(4 * c + ps) * 12 + 2 * 4 + h + 1]
                    b.cop("vector", lambda e, ps=ps, lcol2=lcol2: e.tensor_scalar(
                        u216[:, ps * 512:(ps + 1) * 512], t1f[:, ps * 512:(ps + 1) * 512],
                        lcol2, None, ALU.mult),
                        wait=[("v", b.cnt["v"])] if ps == 0 else None,
                        inc="v" if ps == 3 else None)
                b.cop("vector", lambda e: e.tensor_mul(t1f, dhf, g2f), inc="v")
                b.cop("vector", lambda e: e.tensor_mul(t1f, t1f, t3f),
                      wait=[("v", b.cnt["v"])], inc="v")
                for ps in range(4):
                    lcol0 = l_slab[:, (4 * c + ps) * 12 + 0 * 4 + h:(4 * c + ps) * 12 + 0 * 4 + h + 1]
                    b.cop("vector", lambda e, ps=ps, lcol0=lcol0: e.tensor_scalar(
                        u016[:, ps * 512:(ps + 1) * 512], t1f[:, ps * 512:(ps + 1) * 512],
                        lcol0, None, ALU.mult),
                        wait=[("v", b.cnt["v"])] if ps == 0 else None,
                        inc="v" if ps == 3 else None)
                for ps in range(4):
                    lcol1 = l_slab[:, (4 * c + ps) * 12 + 1 * 4 + h:(4 * c + ps) * 12 + 1 * 4 + h + 1]
                    w = [tml] if ps == 0 else None
                    b.cop("vector", lambda e, ps=ps, lcol1=lcol1: e.tensor_scalar(
                        lv16[:, ps * 512:(ps + 1) * 512], vctm[:, ps * 512:(ps + 1) * 512],
                        lcol1, None, ALU.mult),
                        wait=w, inc="v" if ps == 3 else None)
                ew_done_v = b.cnt["v"]

                oc_t_mark = b.cnt["t"]
                oc_dma_mark = None
                for ps in range(4):
                    bank = pb[4 + ps % 2]
                    for ec in range(4):
                        w = None
                        if ec == 0:
                            w = [("v", hq_ready_v), ("v", b.cnt["v"]), ("s", b.cnt["s"])]
                        b.cop("tensor", lambda e, ps=ps, ec=ec, bank=bank: e.matmul(
                            bank, hq16[:, ec * 512 + ps * 128:ec * 512 + (ps + 1) * 128],
                            w1T16[:, ec * 512:(ec + 1) * 512],
                            start=(ec == 0), stop=(ec == 3)),
                            wait=w, inc="t" if ec == 3 else None)
                    tmk = b.cnt["t"]
                    b.cop("scalar", lambda e, ps=ps, bank=bank: e.activation(
                        sqs[:, 0:512], bank, AF.Square, accum_out=stats[:, 64:65]),
                        wait=[("t", tmk)], inc="s")
                    b.cop("vector", lambda e: e.tensor_scalar(
                        stats[:, 65:66], stats[:, 64:65], 1.0 / FD, 1e-6, ALU.mult, ALU.add),
                        wait=[("s", b.cnt["s"])], inc="v")
                    b.cop("scalar", lambda e: e.activation(stats[:, 66:67], stats[:, 65:66], AF.Ln),
                          wait=[("v", b.cnt["v"])], inc="s")
                    b.cop("scalar", lambda e: e.activation(stats[:, 67:68], stats[:, 66:67],
                                                           AF.Exp, scale=-0.5),
                          wait=[("s", b.cnt["s"])], inc="s")
                    ocw = [("s", b.cnt["s"])]
                    if oc_dma_mark:
                        ocw.append(oc_dma_mark)
                    b.cop("vector", lambda e, bank=bank: e.scalar_tensor_tensor(
                        ocn16[:, 0:512], bank, stats[:, 67:68], tttw_b, ALU.mult, ALU.mult),
                        wait=ocw, inc="v")
                    b.sdma("sync", lambda e, h=h, c=c, ps=ps: e.dma_start(
                        out=ottt_scr[c * 512 + ps * 128:c * 512 + (ps + 1) * 128,
                                     h * FD:(h + 1) * FD],
                        in_=ocn16[:, 0:512]), "so2", wait=[("v", b.cnt["v"])])
                    oc_dma_mark = b.mark("so2")
                    oc_t_mark = b.cnt["t"]

                upd = (
                    (kctm, u016, w0T_f, w0T16),
                    (kctm, u216, w2T_f, w2T16),
                    (lv16, hid16, w1_f, w1_16),
                    (hid16, lv16, w1T_f, w1T16),
                )
                for (lhsU, rhsU, mF, m16) in upd:
                    for dsub in range(4):
                        bank = pb[6]
                        for pc in range(4):
                            w = None
                            if pc == 0:
                                w = [("v", ew_done_v), ("t", oc_t_mark), ("v", b.cnt["v"])]
                            b.cop("tensor", lambda e, dsub=dsub, pc=pc, lhsU=lhsU, rhsU=rhsU, bank=bank: e.matmul(
                                bank, lhsU[:, pc * 512 + dsub * 128:pc * 512 + (dsub + 1) * 128],
                                rhsU[:, pc * 512:(pc + 1) * 512],
                                start=(pc == 0), stop=(pc == 3)),
                                wait=w, inc="t" if pc == 3 else None)
                        b.cop("vector", lambda e, dsub=dsub, mF=mF, bank=bank: e.tensor_tensor(
                            mF[:, dsub * 512:(dsub + 1) * 512], mF[:, dsub * 512:(dsub + 1) * 512],
                            bank, ALU.add),
                            wait=[("t", b.cnt["t"])], inc="v")
                        b.cop("vector", lambda e, dsub=dsub, mF=mF, m16=m16: e.tensor_copy(
                            m16[:, dsub * 512:(dsub + 1) * 512], mF[:, dsub * 512:(dsub + 1) * 512]),
                            wait=[("v", b.cnt["v"])], inc="v")
                st16_ready_v = b.cnt["v"]

    def p5_attn():
        pquad = PS[:, 0:2048]
        sTb = [PS[:, 2048:2560], PS[:, 2560:3072]]
        pout = PS[:, 3072:3584]
        qa_fm = G[:, 0:2048]
        ka_fm = G[:, 2048:4096]
        va = G[:, 4096:4096 + 16 * 128]
        m16 = G[:, 6144:6160]
        mrow16 = G[0:1, 6400:8448]
        expT = [G[:, 8448 + j * 512:8448 + (j + 1) * 512] for j in range(2)]
        outsb = G[:, 9472:9984]
        rl16 = G[0:1, 9984:10496]
        maskAb = G[:, 10496:10496 + 4 * 512]
        maskBb = G[:, 12544:12544 + 4 * 512]
        mstg = F[:, 0:16]
        rlrow = F[0:1, 512:1024]
        rbc = F[:, 1024:1536]
        iot = F[:, 1536:2048 + 512]

        b.cop("gpsimd", lambda e: e.iota(iot[:, 0:512], [[1, 512]], channel_multiplier=-1,
                                         allow_small_or_imprecise_dtypes=True), inc="g")
        gv = b.cnt["g"]
        for v4 in range(4):
            b.cop("vector", lambda e, v4=v4: e.tensor_scalar(
                maskAb[:, v4 * 512:(v4 + 1) * 512], iot[:, 0:512],
                float(v4 * 128), None, ALU.is_gt),
                wait=[("g", gv)] if v4 == 0 else None)
            b.cop("vector", lambda e, v4=v4: e.tensor_scalar(
                maskAb[:, v4 * 512:(v4 + 1) * 512], maskAb[:, v4 * 512:(v4 + 1) * 512],
                -1e9, None, ALU.mult))
            b.cop("vector", lambda e, v4=v4: e.tensor_scalar(
                maskBb[:, v4 * 512:(v4 + 1) * 512], iot[:, 0:512],
                float(v4 * 128), None, ALU.is_lt))
            b.cop("vector", lambda e, v4=v4: e.tensor_scalar(
                maskBb[:, v4 * 512:(v4 + 1) * 512], maskBb[:, v4 * 512:(v4 + 1) * 512],
                -1e9, None, ALU.mult), inc="v" if v4 == 3 else None)
        mask_ready = b.cnt["v"]

        ld_marks = [None, None, None]
        for h in range(NH):
            w = [ld_marks[0]] if ld_marks[0] else None
            b.sdma("sync", lambda e, h=h: e.dma_start_transpose(
                qa_fm, qa_scr[:, h * 128:(h + 1) * 128]), "h10", wait=w)
            qam = b.mark("h10")
            w = [ld_marks[1]] if ld_marks[1] else None
            b.sdma("sync", lambda e, h=h: e.dma_start_transpose(
                ka_fm, ka_scr[:, h * 128:(h + 1) * 128]), "h11", wait=w)
            kam = b.mark("h11")
            w = [ld_marks[2]] if ld_marks[2] else None
            for kt in range(16):
                b.sdma("sync", lambda e, h=h, kt=kt: e.dma_start_transpose(
                    va[:, kt * 128:(kt + 1) * 128],
                    v_scr[h * 128:(h + 1) * 128, kt * 128:(kt + 1) * 128]),
                    "wt0", wait=w if kt == 0 else None)
                w = None
            vam = b.mark("wt0")

            red_mark = b.cnt["v"]
            for qg in range(NT):
                nk = qg // 4 + 1
                mw = [qam, kam, ("v", red_mark), ("v", mask_ready)]
                for kc in range(nk):
                    diag = (kc == nk - 1)
                    b.cop("tensor", lambda e, qg=qg, kc=kc, diag=diag: e.matmul(
                        pquad[:, kc * 512:(kc + 1) * 512],
                        qa_fm[:, qg * 128:(qg + 1) * 128], ka_fm[:, kc * 512:(kc + 1) * 512],
                        start=True, stop=(not diag)),
                        wait=mw if kc == 0 else None,
                        inc=None if diag else ("t" if kc == nk - 1 else None))
                    if diag:
                        v4 = qg % 4
                        b.cop("tensor", lambda e, qg=qg, kc=kc, v4=v4: e.matmul(
                            pquad[:, kc * 512:(kc + 1) * 512], ident[:],
                            maskAb[:, v4 * 512:(v4 + 1) * 512],
                            start=False, stop=True), inc="t")
                b.cop("vector", lambda e, qg=qg, nk=nk: e.reduce_max(
                    mstg[:, qg:qg + 1], pquad[:, 0:nk * 512], AX.X),
                    wait=[("t", b.cnt["t"])], inc="v")
                red_mark = b.cnt["v"]
            b.cop("vector", lambda e: e.tensor_scalar(m16, mstg[:, 0:16], -1.0, None, ALU.mult),
                  wait=[("v", red_mark)], inc="v")
            mv = b.cnt["v"]
            for half in range(2):
                for q8 in range(8):
                    qg = half * 8 + q8
                    w = [("v", mv)] if qg == 0 else ([("v", b.cnt["v"])] if q8 == 0 else None)
                    b.cop("tensor", lambda e, qg=qg, q8=q8: e.transpose(
                        pbt[0:1, q8 * 128:(q8 + 1) * 128], m16[:, qg:qg + 1], ident[:]),
                        wait=w, inc="t" if q8 == 7 else None)
                b.cop("vector", lambda e, half=half: e.tensor_copy(
                    mrow16[:, half * 1024:(half + 1) * 1024], pbt[0:1, 0:1024]),
                    wait=[("t", b.cnt["t"])], inc="v")
            mrow_ready = b.cnt["v"]

            exp_marks = [b.cnt["s"], b.cnt["s"]]
            den_marks = [0, 0]
            pv_done_v = b.cnt["v"]
            den_done_v = b.cnt["v"]
            for ss in range(NS):
                nkt = 4 * (ss + 1)
                for kt in range(nkt):
                    par = kt % 2
                    bank = sTb[par]
                    w = [("s", exp_marks[par]), ("v", mrow_ready)]
                    if kt == 0:
                        w += [("v", pv_done_v), ("v", den_done_v)]
                    b.cop("tensor", lambda e, h=h, ss=ss, kt=kt, bank=bank: e.matmul(
                        bank, ka_fm[:, kt * 128:(kt + 1) * 128],
                        qa_fm[:, ss * 512:(ss + 1) * 512], start=True, stop=False),
                        wait=w)
                    diag = kt >= 4 * ss
                    b.cop("tensor", lambda e, ss=ss, bank=bank, diag=diag: e.matmul(
                        bank, ones1[:], mrow16[:, ss * 512:(ss + 1) * 512],
                        start=False, stop=(not diag)),
                        inc=None if diag else "t")
                    if diag:
                        w4 = kt - 4 * ss
                        b.cop("tensor", lambda e, w4=w4, bank=bank: e.matmul(
                            bank, ident[:], maskBb[:, w4 * 512:(w4 + 1) * 512],
                            start=False, stop=True), inc="t")
                    tmk = b.cnt["t"]
                    ew = [("t", tmk)]
                    if den_marks[par]:
                        ew.append(("t", den_marks[par]))
                    b.cop("scalar", lambda e, par=par, bank=bank: e.activation(
                        expT[par], bank, AF.Exp), wait=ew, inc="s")
                    exp_marks[par] = b.cnt["s"]
                    pvw = [("s", exp_marks[par])]
                    b.cop("tensor", lambda e, kt=kt, par=par, nkt=nkt: e.matmul(
                        pout, va[:, kt * 128:(kt + 1) * 128], expT[par],
                        start=(kt == 0), stop=(kt == nkt - 1)), wait=pvw)
                    b.cop("tensor", lambda e, kt=kt, par=par, nkt=nkt: e.matmul(
                        pquad[0:1, 0:512], onesc[:], expT[par],
                        start=(kt == 0), stop=(kt == nkt - 1), skip_group_check=True),
                        inc="t")
                    den_marks[par] = b.cnt["t"]
                tmk = b.cnt["t"]
                b.cop("vector", lambda e: e.tensor_copy(rlrow, pquad[0:1, 0:512]),
                      wait=[("t", tmk)], inc="v")
                den_done_v = b.cnt["v"]
                b.cop("vector", lambda e: e.reciprocal(rlrow, rlrow),
                      wait=[("v", b.cnt["v"])], inc="v")
                b.cop("vector", lambda e: e.tensor_copy(rl16[:], rlrow),
                      wait=[("v", b.cnt["v"])], inc="v")
                b.cop("tensor", lambda e: e.matmul(
                    pquad[:, 512:1024], ones1[:], rl16[:], start=True, stop=True,
                    skip_group_check=True), wait=[("v", b.cnt["v"])], inc="t")
                b.cop("vector", lambda e: e.tensor_copy(rbc, pquad[:, 512:1024]),
                      wait=[("t", b.cnt["t"])], inc="v")
                b.cop("vector", lambda e: e.tensor_tensor(outsb, pout, rbc, ALU.mult), inc="v")
                pv_done_v = b.cnt["v"]
                b.sdma("sync", lambda e, h=h, ss=ss: e.dma_start(
                    out=attn_scr[h * 128:(h + 1) * 128, ss * 512:(ss + 1) * 512],
                    in_=outsb), "so0", wait=[("v", b.cnt["v"])])
            ld_marks = [("v", b.cnt["v"]), ("v", b.cnt["v"]), ("v", b.cnt["v"])]

    def bcast(r, dst):
        b.sdma("sync", lambda e, r=r: e.dma_start(out=srow[:], in_=rowbank[r:r + 1, :]),
               "sr", wait=[("v", b.cnt["v"])])
        b.cop("vector", lambda e: e.tensor_copy(srow16[:], srow[:]),
              wait=[b.mark("sr")], inc="v")
        sv = b.cnt["v"]
        for j in range(4):
            bank = pb[4 + (j % 2)]
            w = [("v", sv)]
            if j >= 2:
                w.append(("v", b.cnt["v"]))
            b.cop("tensor", lambda e, j=j, bank=bank: e.matmul(
                bank[:, :], ones1[:], srow16[:, j * 512:(j + 1) * 512],
                start=True, stop=True), wait=w, inc="t")
            b.cop("vector", lambda e, j=j, bank=bank: e.tensor_copy(
                dst[:, j * 512:(j + 1) * 512], bank[:, :]),
                wait=[("t", b.cnt["t"])], inc="v")

    def p1_ln1():
        xdump = dbg_outs.get("dbg_xt2")
        smdump = dbg_outs.get("dbg_sm")
        xt = [F[:, j * H:(j + 1) * H] for j in range(2)]
        ut = F[:, 2 * H:3 * H]
        xsq = F[:, 3 * H:4 * H]
        xc = F[:, 4 * H:5 * H]
        h1t = [G[:, j * H:(j + 1) * H] for j in range(2)]
        bcast(R_ONEPCA, bpool[0])
        bcast(R_SA, bpool[1])
        bc = ("v", b.cnt["v"])
        sh_mark = {0: None, 1: None}
        for i in range(NT):
            xb, hb = xt[i % 2], h1t[i % 2]
            par = i % 2
            xsem = "x%d" % par
            shsem = "sh%d" % par
            sm = stats[:, (i % 4) * 8:(i % 4) * 8 + 8]
            w = [sh_mark[(i - 1) % 2]] if i >= 2 else None
            b.sdma("sync", lambda e, i=i, xb=xb: e.dma_start(
                out=xb, in_=x_in[i * 128:(i + 1) * 128, :]), xsem, wait=w)
            lv = b.mark(xsem)
            if i == 2 and xdump is not None:
                b.sdma("sync", lambda e, xb=xb: e.dma_start(out=xdump[0:128, :], in_=xb),
                       "st", wait=[lv])
            b.cop("vector", lambda e, xb=xb, sm=sm: e.reduce_sum(sm[:, 0:1], xb, AX.X),
                  wait=[lv, bc], inc="v")
            rsv = b.cnt["v"]
            b.cop("scalar", lambda e, xb=xb, sm=sm: e.activation(xsq, xb, AF.Square, accum_out=sm[:, 1:2]),
                  wait=[lv], inc="s")
            b.cop("vector", lambda e, sm=sm: e.tensor_scalar(sm[:, 2:3], sm[:, 0:1], 1.0 / H, None, ALU.mult),
                  wait=[("s", b.cnt["s"]), ("v", rsv)], inc="v")
            b.cop("vector", lambda e, sm=sm: e.tensor_mul(sm[:, 3:4], sm[:, 2:3], sm[:, 2:3]),
                  wait=[("v", b.cnt["v"])], inc="v")
            b.cop("vector", lambda e, sm=sm: e.tensor_scalar(sm[:, 4:5], sm[:, 1:2], 1.0 / H, None, ALU.mult),
                  wait=[("v", b.cnt["v"])], inc="v")
            b.cop("vector", lambda e, sm=sm: e.tensor_sub(sm[:, 4:5], sm[:, 4:5], sm[:, 3:4]),
                  wait=[("v", b.cnt["v"])], inc="v")
            b.cop("vector", lambda e, sm=sm: e.tensor_scalar(sm[:, 4:5], sm[:, 4:5], 1e-5, None, ALU.add),
                  wait=[("v", b.cnt["v"])], inc="v")
            b.cop("scalar", lambda e, sm=sm: e.activation(sm[:, 5:6], sm[:, 4:5], AF.Ln),
                  wait=[("v", b.cnt["v"])], inc="s")
            b.cop("scalar", lambda e, sm=sm: e.activation(sm[:, 6:7], sm[:, 5:6], AF.Exp, scale=-0.5),
                  wait=[("s", b.cnt["s"])], inc="s")
            b.cop("vector", lambda e, xb=xb, sm=sm: e.tensor_scalar(
                xc, xb, sm[:, 2:3], None, ALU.subtract),
                wait=[("v", b.cnt["v"])], inc="v")
            b.cop("scalar", lambda e, sm=sm: e.activation(
                ut, xc, AF.Identity, scale=sm[:, 6:7]),
                wait=[("v", b.cnt["v"])], inc="s")
            if smdump is not None:
                b.sdma("sync", lambda e, i=i, sm=sm: e.dma_start(
                    out=smdump[:, i * 8:i * 8 + 7], in_=sm[:, 0:7]),
                    "st", wait=[("s", b.cnt["s"])])
            b.cop("vector", lambda e: e.tensor_mul(ut, ut, bpool[0][:]),
                  wait=[("s", b.cnt["s"])])
            hw = [sh_mark[par]] if i >= 2 else None
            b.cop("vector", lambda e, hb=hb: e.tensor_add(hb, ut, bpool[1][:]), wait=hw, inc="v")
            b.sdma("sync", lambda e, i=i, hb=hb: e.dma_start(
                out=h_scr[i * 128:(i + 1) * 128, :], in_=hb),
                shsem, wait=[("v", b.cnt["v"])])
            sh_mark[par] = b.mark(shsem)

    def p2_qkv():
        import os
        lr_en = os.environ.get("LR_EN", "1") == "1"
        sq_en = os.environ.get("SQ_EN", "1") == "1"
        tl_en = os.environ.get("TL_EN", "1") == "1"
        cast_en = os.environ.get("CAST_EN", "1") == "1"
        if cast_en:
            bounce = [G[:, j * 2048:(j + 1) * 2048] for j in range(2)]
            stm = {0: None, 1: None}
            for j in range(48):
                buf = bounce[j % 2]
                w = [stm[j % 2]] if stm[j % 2] else None
                b.sdma("gpsimd", lambda e, j=j, buf=buf: e.dma_start(
                    out=buf, in_=qkv_w[j * 128:(j + 1) * 128, :]), "dt", wait=w)
                b.sdma("sync", lambda e, j=j, buf=buf: e.dma_start(
                    out=qkv16_scr[j * 128:(j + 1) * 128, :], in_=buf), "so3",
                    wait=[b.mark("dt")])
                stm[j % 2] = b.mark("so3")
        castdone = b.mark("so3")
        qkv16 = G[:, 0:24576]
        pre = G[:, 24576:32768]
        h1fm3 = G[:, 32768:40960].rearrange("p (c n) -> p c n", c=16)
        wTb = [G[:, 40960 + j * 768:40960 + (j + 1) * 768] for j in range(2)]
        sq16 = G[:, 42496:43008]
        outst = G[:, 43008:44544]
        row16 = G[0:1, 44544:45056]
        rstq = F[:, 0:512]
        rstk = F[:, 512:1024]
        rst2 = F[:, 1024:1536]
        rowq = F[0:1, 1536:2048]
        rowk = F[0:1, 2048:2560]
        row2 = F[0:1, 2560:3072]
        lrraw = F[0:12, 3072:3584]
        lrtmp = F[0:12, 3584:4096]
        lrtmp2 = F[0:12, 4096:4608]

        def qkv_c(fc):
            return qkv16[:, fc * 512:(fc + 1) * 512]

        wT_marks = {}
        for s in range(NS):
            stripe_st0 = [b.mark("so0"), b.mark("so1"), b.mark("so2"), b.mark("so3")]
            h1w = [("t", b.cnt["t"])] if s > 0 else None
            for hc in range(NF):
                if tl_en:
                    b.sdma("sync", lambda e, s=s, hc=hc: e.dma_start_transpose(
                        h1fm3[:, hc, :],
                        h_scr[s * 512:(s + 1) * 512, hc * 128:(hc + 1) * 128]),
                        "h1%d" % (hc % 2), wait=h1w if hc == 0 else None)
                else:
                    b.sdma("sync", lambda e, s=s, hc=hc: e.dma_start(
                        out=h1fm3[:, hc, :],
                        in_=h_scr[s * 512 + (hc % 4) * 128:s * 512 + (hc % 4) * 128 + 128, (hc % 4) * 512:(hc % 4) * 512 + 512]),
                        "h1%d" % (hc % 2), wait=h1w if hc == 0 else None)
            h1v = [b.mark("h10"), b.mark("h11")]
            evac_mark = b.cnt["v"]
            sq_mm_mark = b.cnt["t"]
            for grp in range(8):
                for hc in range(NF):
                    it = grp * NF + hc
                    buf = wTb[it % 2]
                    wtsem = "wt%d" % (it % 2)
                    w = [castdone]
                    if (s, it - 2) in wT_marks or (s > 0 and it < 2):
                        prev = wT_marks.get((s, it - 2), wT_marks.get((s - 1, 126 + it)))
                        if prev is not None:
                            w.append(("t", prev))
                    b.sdma("sync", lambda e, grp=grp, hc=hc, buf=buf: e.dma_start_transpose(
                        buf, qkv16_scr[grp * 768:(grp + 1) * 768, hc * 128:(hc + 1) * 128]),
                        wtsem, wait=w)
                    mmw = [b.mark(wtsem)] + h1v
                    if hc == 0:
                        mmw.append(("v", evac_mark))
                        mmw.append(("s", b.cnt["s"]))
                    for fc6 in range(6):
                        b.cop("tensor", lambda e, hc=hc, fc6=fc6, buf=buf: e.matmul(
                            pb[fc6][:, :], buf[:, fc6 * 128:(fc6 + 1) * 128],
                            h1fm3[:, hc, :],
                            start=(hc == 0), stop=(hc == NF - 1)),
                            wait=mmw if fc6 == 0 else None,
                            inc="t" if fc6 == 5 else None)
                    wT_marks[(s, it)] = b.cnt["t"]
                if grp == 0 and lvl >= 2 and lr_en:
                    lrw_w = [("v", b.cnt["v"])]
                    for hc in range(NF):
                        b.cop("tensor", lambda e, hc=hc: e.matmul(
                            pb[6][64:76, :], lrwT16[:, hc * 12:(hc + 1) * 12],
                            h1fm3[:, hc, :],
                            start=(hc == 0), stop=(hc == NF - 1),
                            skip_group_check=True),
                            wait=lrw_w if hc == 0 else None,
                            inc="t" if hc == NF - 1 else None)
                tv = b.cnt["t"]
                lr_evac_v = None
                if grp == 0 and lvl >= 2 and lr_en:
                    b.cop("vector", lambda e: e.tensor_copy(lrraw[:, :], pb[6][64:76, :]),
                          wait=[("t", tv)] + (stripe_st0 if s > 0 else []), inc="v")
                    lr_evac_v = b.cnt["v"]
                for fc6 in range(6):
                    fc = grp * 6 + fc6
                    ew = None
                    if fc6 == 0:
                        ew = [("t", tv)]
                        if grp == 0 and s > 0:
                            ew.extend(stripe_st0)
                    b.cop("vector", lambda e, fc=fc, fc6=fc6: e.tensor_copy(
                        qkv16[:, fc * 512:(fc + 1) * 512], pb[fc6][:, :]),
                        wait=ew, inc="v")
                    cp_mark = b.cnt["v"]
                    if fc < 32 and lvl >= 2 and sq_en:
                        b.cop("scalar", lambda e, fc=fc: e.activation(
                            sq16[:, 0:512], qkv16[:, fc * 512:(fc + 1) * 512], AF.Square),
                            wait=[("v", cp_mark), ("t", sq_mm_mark)], inc="s")
                        rowoff = 0 if fc < 16 else 32
                        mw = [("s", b.cnt["s"])]
                        if lr_evac_v is not None and fc == 0:
                            mw.append(("v", lr_evac_v))
                        b.cop("tensor", lambda e, rowoff=rowoff, fc=fc: e.matmul(
                            pb[6][rowoff:rowoff + 1, :], onesc[:], sq16[:, 0:512],
                            start=(fc % 16 == 0), stop=(fc % 16 == 15),
                            skip_group_check=True),
                            wait=mw, inc="t")
                        sq_mm_mark = b.cnt["t"]
                evac_mark = b.cnt["v"]
            mmdone = b.cnt["t"]
            vdone = b.cnt["v"]
            if lvl < 3:
                continue

            b.cop("scalar", lambda e: e.activation(lrtmp, lrraw, AF.Identity, bias=lrb_col[:]),
                  wait=[("v", vdone)], inc="s")
            b.cop("scalar", lambda e: e.activation(lrtmp2, lrtmp, AF.Abs),
                  wait=[("s", b.cnt["s"])], inc="s")
            b.cop("scalar", lambda e: e.activation(lrtmp2, lrtmp2, AF.Exp, scale=-1.0),
                  wait=[("s", b.cnt["s"])], inc="s")
            b.cop("scalar", lambda e: e.activation(lrtmp2, lrtmp2, AF.Ln, bias=1.0),
                  wait=[("s", b.cnt["s"])], inc="s")
            b.cop("vector", lambda e: e.tensor_scalar(lrtmp, lrtmp, 0.0, None, ALU.max),
                  wait=[("s", b.cnt["s"])])
            b.cop("vector", lambda e: e.tensor_add(lrtmp, lrtmp, lrtmp2), inc="v")
            lr16 = G[0:12, 44544:45056]
            b.cop("vector", lambda e, lr16=lr16: e.tensor_copy(lr16, lrtmp), inc="v")
            sv = b.cnt["v"]
            for q4 in range(4):
                b.cop("tensor", lambda e, q4=q4, lr16=lr16: e.transpose(
                    pbt[:, q4 * 12:(q4 + 1) * 12], lr16[:, q4 * 128:(q4 + 1) * 128],
                    ident[0:12, 0:12]),
                    wait=[("v", sv)] if q4 == 0 else None,
                    inc="t" if q4 == 3 else None)
            b.cop("vector", lambda e, s=s: e.tensor_copy(
                l_slab[:, s * 48:(s + 1) * 48], pbt[:, 0:48]),
                wait=[("t", b.cnt["t"])], inc="v")

            b.cop("vector", lambda e: e.tensor_scalar(rowq, pb[6][0:1, 0:512], 1.0 / H, 1e-6, ALU.mult, ALU.add),
                  wait=[("t", mmdone)])
            b.cop("vector", lambda e: e.tensor_scalar(rowk, pb[6][32:33, 0:512], 1.0 / H, 1e-6, ALU.mult, ALU.add), inc="v")
            b.cop("scalar", lambda e: e.activation(rowq, rowq, AF.Ln), wait=[("v", b.cnt["v"])], inc="s")
            b.cop("scalar", lambda e: e.activation(rowq, rowq, AF.Exp, scale=-0.5),
                  wait=[("s", b.cnt["s"])], inc="s")
            b.cop("scalar", lambda e: e.activation(rowk, rowk, AF.Ln),
                  wait=[("s", b.cnt["s"])], inc="s")
            b.cop("scalar", lambda e: e.activation(rowk, rowk, AF.Exp, scale=-0.5),
                  wait=[("s", b.cnt["s"])], inc="s")
            sv = b.cnt["s"]
            lr_tr_mark = b.cnt["t"]
            for (rowx, dstx, bankx) in ((rowq, rstq, 2), (rowk, rstk, 3)):
                b.cop("vector", lambda e, rowx=rowx: e.tensor_copy(row16[:], rowx),
                      wait=[("s", sv), ("t", lr_tr_mark)], inc="v")
                b.cop("tensor", lambda e, bankx=bankx: e.matmul(
                    pb[bankx][:, :], ones1[:], row16[:], start=True, stop=True),
                    wait=[("v", b.cnt["v"])], inc="t")
                b.cop("vector", lambda e, dstx=dstx, bankx=bankx: e.tensor_copy(dstx, pb[bankx][:, :]),
                      wait=[("t", b.cnt["t"])], inc="v")
            rst_ready = b.cnt["v"]
            if lvl < 4:
                continue

            silu_mark = b.cnt["s"]
            qn_dma_mark = b.mark("so0")
            f_dma_mark = b.mark("so1")
            hs_mm_mark = b.cnt["t"]
            for xk in range(2):
                rstx = rstq if xk == 0 else rstk
                ucol, ocol, nwcol = (C_U0, C_O0, C_QNW) if xk == 0 else (C_U1, C_O1, C_KNW)
                nscr = qn_scr if xk == 0 else kn_scr
                fscr = fq_scr if xk == 0 else fk_scr
                for hd4 in range(NFW):
                    for c4 in range(4):
                        fc = xk * 16 + hd4 * 4 + c4
                        fcl = hd4 * 4 + c4
                        w = [qn_dma_mark]
                        if fc == 0:
                            w.append(("v", rst_ready))
                        b.cop("vector", lambda e, fc=fc, fcl=fcl, rstx=rstx, nwcol=nwcol: e.scalar_tensor_tensor(
                            outst[:, 0:512], qkv_c(fc), colbank[:, nwcol * 16 + fcl:nwcol * 16 + fcl + 1],
                            rstx, ALU.mult, ALU.mult), wait=w, inc="v")
                        b.sdma("sync", lambda e, fcl=fcl, s=s, nscr=nscr: e.dma_start(
                            out=nscr[fcl * 128:(fcl + 1) * 128, s * 512:(s + 1) * 512],
                            in_=outst[:, 0:512]),
                            "so0", wait=[("v", b.cnt["v"])])
                        qn_dma_mark = b.mark("so0")
                        b.cop("vector", lambda e, fc=fc, fcl=fcl, rstx=rstx, ucol=ucol: e.scalar_tensor_tensor(
                            outst[:, 512:1024], qkv_c(fc), colbank[:, ucol * 16 + fcl:ucol * 16 + fcl + 1],
                            rstx, ALU.mult, ALU.mult),
                            wait=[("s", silu_mark)], inc="v")
                        b.cop("scalar", lambda e, fcl=fcl, ocol=ocol: e.activation(
                            pre[:, fcl * 512:(fcl + 1) * 512], outst[:, 512:1024], AF.Silu,
                            bias=colbank[:, ocol * 16 + fcl:ocol * 16 + fcl + 1]),
                            wait=[("v", b.cnt["v"])], inc="s")
                        silu_mark = b.cnt["s"]
                        b.cop("scalar", lambda e, fcl=fcl: e.activation(
                            sq16[:, 0:512], pre[:, fcl * 512:(fcl + 1) * 512], AF.Square),
                            wait=[("t", hs_mm_mark)], inc="s")
                        b.cop("tensor", lambda e, hd4=hd4, c4=c4: e.matmul(
                            pb[hd4 % 2][0:1, :], onesc[:], sq16[:, 0:512],
                            start=(c4 == 0), stop=(c4 == 3),
                            skip_group_check=True),
                            wait=[("s", b.cnt["s"])], inc="t")
                        hs_mm_mark = b.cnt["t"]
                    b.cop("vector", lambda e, hd4=hd4: e.tensor_scalar(
                        row2, pb[hd4 % 2][0:1, :], 1.0, 1e-12, ALU.mult, ALU.add),
                        wait=[("t", hs_mm_mark)], inc="v")
                    b.cop("scalar", lambda e: e.activation(row2, row2, AF.Ln),
                          wait=[("v", b.cnt["v"])], inc="s")
                    b.cop("scalar", lambda e: e.activation(row2, row2, AF.Exp, scale=-0.5),
                          wait=[("s", b.cnt["s"])], inc="s")
                    b.cop("vector", lambda e: e.tensor_copy(row16[:], row2),
                          wait=[("s", b.cnt["s"])], inc="v")
                    b.cop("tensor", lambda e, hd4=hd4: e.matmul(
                        pb[4 + hd4 % 2][:, :], ones1[:], row16[:], start=True, stop=True),
                        wait=[("v", b.cnt["v"])], inc="t")
                    b.cop("vector", lambda e, hd4=hd4: e.tensor_copy(rst2, pb[4 + hd4 % 2][:, :]),
                          wait=[("t", b.cnt["t"])], inc="v")
                    for c4 in range(4):
                        fcl = hd4 * 4 + c4
                        b.cop("vector", lambda e, fcl=fcl: e.tensor_mul(
                            outst[:, 1024:1536], pre[:, fcl * 512:(fcl + 1) * 512], rst2),
                            wait=[f_dma_mark], inc="v")
                        b.sdma("sync", lambda e, fcl=fcl, s=s, fscr=fscr: e.dma_start(
                            out=fscr[fcl * 128:(fcl + 1) * 128, s * 512:(s + 1) * 512],
                            in_=outst[:, 1024:1536]),
                            "so1", wait=[("v", b.cnt["v"])])
                        f_dma_mark = b.mark("so1")
            if lvl < 5:
                continue
            fv_mark = b.mark("so2")
            for fcl in range(16):
                fc = 32 + fcl
                b.cop("scalar", lambda e, fc=fc: e.activation(
                    outst[:, 512:1024], qkv_c(fc), AF.Silu),
                    wait=[fv_mark], inc="s")
                b.sdma("sync", lambda e, fcl=fcl, s=s: e.dma_start(
                    out=fv_scr[fcl * 128:(fcl + 1) * 128, s * 512:(s + 1) * 512],
                    in_=outst[:, 512:1024]),
                    "so2", wait=[("s", b.cnt["s"])])
                fv_mark = b.mark("so2")
                b.sdma("sync", lambda e, fcl=fcl, fc=fc, s=s: e.dma_start(
                    out=v_scr[fcl * 128:(fcl + 1) * 128, s * 512:(s + 1) * 512],
                    in_=qkv_c(fc)), "so3")

    def p6_oproj():
        acts16 = G[:, 0:8192]
        atmp = [G[:, 8192 + j * 512:8192 + (j + 1) * 512] for j in range(2)]
        otmp = [G[:, 9216 + j * 512:9216 + (j + 1) * 512] for j in range(2)]
        wTb6 = [G[:, 10240 + j * 768:10240 + (j + 1) * 768] for j in range(2)]
        outst6 = [G[:, 11776 + j * 512:11776 + (j + 1) * 512] for j in range(2)]
        bounce = [G[:, 12800 + j * 2048:12800 + (j + 1) * 2048] for j in range(2)]
        stm = {0: None, 1: None}
        for j in range(16):
            buf = bounce[j % 2]
            w = [stm[j % 2]] if stm[j % 2] else None
            b.sdma("gpsimd", lambda e, j=j, buf=buf: e.dma_start(
                out=buf, in_=wo_in[j * 128:(j + 1) * 128, :]), "dt", wait=w)
            b.sdma("sync", lambda e, j=j, buf=buf: e.dma_start(
                out=wo16_scr[j * 128:(j + 1) * 128, :], in_=buf), "so3",
                wait=[b.mark("dt")])
            stm[j % 2] = b.mark("so3")
        castdone = b.mark("so3")
        GRP6 = ((0, 6), (6, 6), (12, 4))
        wT_marks = {}
        mm_last = b.cnt["t"]
        st_marks = {0: None, 1: None}
        add_v = {0: None, 1: None}
        for ss in range(NS):
            for hc in range(NF):
                par = hc % 2
                ldw = []
                if ss > 0 and hc < 2:
                    ldw.append(("t", mm_last))
                if add_v[par]:
                    ldw.append(add_v[par])
                b.sdma("sync", lambda e, ss=ss, hc=hc, par=par: e.dma_start(
                    out=atmp[par],
                    in_=attn_scr[hc * 128:(hc + 1) * 128, ss * 512:(ss + 1) * 512]),
                    "h1%d" % par, wait=ldw or None)
                am = b.mark("h1%d" % par)
                b.sdma("sync", lambda e, ss=ss, hc=hc, par=par: e.dma_start_transpose(
                    otmp[par], ottt_scr[ss * 512:(ss + 1) * 512, hc * 128:(hc + 1) * 128]),
                    "x%d" % par, wait=ldw or None)
                om = b.mark("x%d" % par)
                addw = [am, om]
                if ss > 0:
                    addw.append(("t", mm_last))
                b.cop("vector", lambda e, hc=hc, par=par: e.tensor_add(
                    acts16[:, hc * 512:(hc + 1) * 512], atmp[par], otmp[par]),
                    wait=addw, inc="v")
                add_v[par] = ("v", b.cnt["v"])
            acts_v = b.cnt["v"]
            evac_v = b.cnt["v"]
            for gi, (fst, ncol) in enumerate(GRP6):
                for hc in range(NF):
                    it = gi * 16 + hc
                    par = it % 2
                    buf = wTb6[par]
                    wtsem = "wt%d" % par
                    w = [castdone]
                    prev = wT_marks.get((ss, it - 2))
                    if prev is None and ss > 0:
                        prev = wT_marks.get((ss - 1, 46 + it))
                    if prev is not None:
                        w.append(("t", prev))
                    b.sdma("sync", lambda e, fst=fst, ncol=ncol, hc=hc, buf=buf: e.dma_start_transpose(
                        buf[:, 0:ncol * 128],
                        wo16_scr[fst * 128:fst * 128 + ncol * 128, hc * 128:(hc + 1) * 128]),
                        wtsem, wait=w)
                    mmw = [b.mark(wtsem), ("v", acts_v)]
                    if hc == 0:
                        mmw.append(("v", evac_v))
                    for fc6 in range(ncol):
                        b.cop("tensor", lambda e, fc6=fc6, hc=hc, buf=buf: e.matmul(
                            pb[fc6][:, :], buf[:, fc6 * 128:(fc6 + 1) * 128],
                            acts16[:, hc * 512:(hc + 1) * 512],
                            start=(hc == 0), stop=(hc == NF - 1)),
                            wait=mmw if fc6 == 0 else None,
                            inc="t" if fc6 == ncol - 1 else None)
                    wT_marks[(ss, it)] = b.cnt["t"]
                tv = b.cnt["t"]
                for fc6 in range(ncol):
                    fc = fst + fc6
                    opar = fc6 % 2
                    ew = [("t", tv)] if fc6 == 0 else []
                    if st_marks[opar]:
                        ew.append(st_marks[opar])
                    b.cop("vector", lambda e, fc=fc, fc6=fc6, opar=opar: e.tensor_scalar(
                        outst6[opar], pb[fc6][:, :],
                        colbank[:, C_GA * 16 + fc:C_GA * 16 + fc + 1], None, ALU.mult),
                        wait=ew or None, inc="v")
                    b.sdma("sync", lambda e, fc=fc, ss=ss, opar=opar: e.dma_start(
                        out=branch_scr[fc * 128:(fc + 1) * 128, ss * 512:(ss + 1) * 512],
                        in_=outst6[opar]), "so%d" % opar, wait=[("v", b.cnt["v"])])
                    st_marks[opar] = b.mark("so%d" % opar)
                evac_v = b.cnt["v"]
            mm_last = b.cnt["t"]

    def p7_ln2():
        xt = [F[:, 0:H], F[:, H:2 * H]]
        x2t = F[:, 2 * H:3 * H]
        ut = F[:, 3 * H:4 * H]
        xsq = F[:, 4 * H:5 * H]
        xc2 = F[:, 5 * H:6 * H]
        h2t = F[:, 6 * H:7 * H]
        at = F[:, 7 * H:8 * H]
        cct = F[:, 8 * H:9 * H]
        btm = [G[:, 0:H], G[:, H:2 * H]]
        mk16 = G[:, 2 * H:3 * H]
        sg16 = G[:, 3 * H:4 * H]
        xq16 = [G[:, 4 * H:5 * H], G[:, 5 * H:6 * H]]
        bcast(R_ONEPCB, bpool[0])
        bcast(R_SB, bpool[1])
        bc = ("v", b.cnt["v"])
        sh_mark = {0: None, 1: None}
        xq_mark = {0: None, 1: None}
        x2add = {0: None, 1: None}
        h2_cons = None
        for i in range(NT):
            par = i % 2
            xb, bb, xqb = xt[par], btm[par], xq16[par]
            sm = stats[:, (i % 4) * 8:(i % 4) * 8 + 8]
            w = [x2add[par]] if x2add[par] else None
            b.sdma("sync", lambda e, i=i, xb=xb: e.dma_start(
                out=xb, in_=x_in[i * 128:(i + 1) * 128, :]), "x%d" % par, wait=w)
            b.sdma("sync", lambda e, i=i, bb=bb: e.dma_start_transpose(
                bb, branch_scr[:, i * 128:(i + 1) * 128]), "h1%d" % par, wait=w)
            lv = [b.mark("x%d" % par), b.mark("h1%d" % par), bc]
            x2w = lv + [("s", b.cnt["s"])]
            if i >= 1:
                x2w.append(sh_mark[(i - 1) % 2])
            b.cop("vector", lambda e, xb=xb, bb=bb: e.tensor_add(x2t, xb, bb),
                  wait=[m for m in x2w if m], inc="v")
            x2add[par] = ("v", b.cnt["v"])
            x2v = ("v", b.cnt["v"])
            b.sdma("sync", lambda e, i=i: e.dma_start(
                out=x2_scr[i * 128:(i + 1) * 128, :], in_=x2t), "sh%d" % par, wait=[x2v])
            sh_mark[par] = b.mark("sh%d" % par)
            b.cop("vector", lambda e, sm=sm: e.reduce_sum(sm[:, 0:1], x2t, AX.X),
                  wait=[x2v], inc="v")
            rsv = b.cnt["v"]
            b.cop("scalar", lambda e, sm=sm: e.activation(xsq, x2t, AF.Square, accum_out=sm[:, 1:2]),
                  wait=[x2v], inc="s")
            b.cop("vector", lambda e, sm=sm: e.tensor_scalar(sm[:, 2:3], sm[:, 0:1], 1.0 / H, None, ALU.mult),
                  wait=[("s", b.cnt["s"]), ("v", rsv)], inc="v")
            b.cop("vector", lambda e, sm=sm: e.tensor_mul(sm[:, 3:4], sm[:, 2:3], sm[:, 2:3]),
                  wait=[("v", b.cnt["v"])], inc="v")
            b.cop("vector", lambda e, sm=sm: e.tensor_scalar(sm[:, 4:5], sm[:, 1:2], 1.0 / H, None, ALU.mult),
                  wait=[("v", b.cnt["v"])], inc="v")
            b.cop("vector", lambda e, sm=sm: e.tensor_sub(sm[:, 4:5], sm[:, 4:5], sm[:, 3:4]),
                  wait=[("v", b.cnt["v"])], inc="v")
            b.cop("vector", lambda e, sm=sm: e.tensor_scalar(sm[:, 4:5], sm[:, 4:5], 1e-5, None, ALU.add),
                  wait=[("v", b.cnt["v"])], inc="v")
            b.cop("scalar", lambda e, sm=sm: e.activation(sm[:, 5:6], sm[:, 4:5], AF.Ln),
                  wait=[("v", b.cnt["v"])], inc="s")
            b.cop("scalar", lambda e, sm=sm: e.activation(sm[:, 6:7], sm[:, 5:6], AF.Exp, scale=-0.5),
                  wait=[("s", b.cnt["s"])], inc="s")
            b.cop("vector", lambda e, sm=sm: e.tensor_scalar(
                xc2, x2t, sm[:, 2:3], None, ALU.subtract),
                wait=[("v", b.cnt["v"]), ("s", b.cnt["s"])], inc="v")
            b.cop("scalar", lambda e, sm=sm: e.activation(
                ut, xc2, AF.Identity, scale=sm[:, 6:7]),
                wait=[("v", b.cnt["v"])], inc="s")
            b.cop("vector", lambda e: e.tensor_mul(ut, ut, bpool[0][:]),
                  wait=[("s", b.cnt["s"])], inc="v")
            h2w = [("v", b.cnt["v"])]
            if h2_cons:
                h2w += h2_cons
            b.cop("vector", lambda e: e.tensor_add(h2t, ut, bpool[1][:]), wait=h2w, inc="v")
            h2v = ("v", b.cnt["v"])
            b.cop("vector", lambda e, sm=sm: e.tensor_reduce(
                sm[:, 7:8], h2t, AX.X, ALU.max, apply_absolute_value=True),
                wait=[h2v], inc="v")
            b.cop("vector", lambda e, i=i, sm=sm: e.tensor_scalar(
                qslab[:, i:i + 1], sm[:, 7:8], 1.0 / 127.0, 1e-5, ALU.mult, ALU.add),
                wait=[("v", b.cnt["v"])], inc="v")
            rcol = stats[:, 96 + par:97 + par]
            b.cop("vector", lambda e, i=i, rcol=rcol: e.reciprocal(rcol, qslab[:, i:i + 1]),
                  wait=[("v", b.cnt["v"])], inc="v")
            b.cop("scalar", lambda e, rcol=rcol: e.activation(at, h2t, AF.Identity, scale=rcol),
                  wait=[("v", b.cnt["v"])], inc="s")
            h2_cons = [("s", b.cnt["s"])]
            b.cop("scalar", lambda e: e.activation(sg16, at, AF.Sign),
                  wait=[("s", b.cnt["s"]), ("v", b.cnt["v"])], inc="s")
            b.cop("vector", lambda e: e.scalar_tensor_tensor(at, at, -1.0, at, ALU.mult, ALU.max),
                  wait=[("s", b.cnt["s"])], inc="v")
            b.cop("vector", lambda e: e.tensor_scalar(mk16, at, 63.5, None, ALU.is_ge))
            b.cop("vector", lambda e: e.tensor_scalar(cct, mk16, 64.0, None, ALU.mult))
            for bit in (32, 16, 8, 4, 2, 1):
                b.cop("vector", lambda e, bb_=float(bit): e.scalar_tensor_tensor(
                    mk16, cct, bb_ - 0.5, at, ALU.add, ALU.is_le))
                b.cop("vector", lambda e, bb_=float(bit): e.scalar_tensor_tensor(
                    cct, mk16, bb_, cct, ALU.mult, ALU.add))
            xqw = [xq_mark[par]] if xq_mark[par] else None
            b.cop("vector", lambda e, xqb=xqb: e.tensor_mul(xqb, cct, sg16), wait=xqw, inc="v")
            b.sdma("sync", lambda e, i=i, xqb=xqb: e.dma_start(
                out=xq_scr[i * 128:(i + 1) * 128, :], in_=xqb),
                "so%d" % par, wait=[("v", b.cnt["v"])])
            xq_mark[par] = b.mark("so%d" % par)

    def p8a_wquant():
        wbuf = [F[:, 0:4096], F[:, 4096:8192]]
        m1 = F[:, 8192:12288]
        m2 = F[:, 12288:16384]
        qs16 = [G[:, j * 4096:(j + 1) * 4096] for j in range(2)]
        for (w_in_, rows, cols, wq_scr_, m) in (
                (wg_in, I, H, wqg_scr, 0), (wu_in, I, H, wqu_scr, 1),
                (wd_in, H, I, wqd_scr, 2)):
            chunks = [(r, c) for r in range(0, rows, 128) for c in range(0, cols, 4096)]
            cw = min(cols, 4096)
            acc = stats[:, 24 + m:25 + m]
            tmp = stats[:, 28 + m:29 + m]
            b.cop("vector", lambda e, acc=acc: e.memset(acc, 0.0), inc="v")
            lm = {0: None, 1: None}
            for j, (r, c) in enumerate(chunks):
                par = j % 2
                buf = wbuf[par][:, 0:cw]
                w = [lm[par]] if lm[par] else None
                b.sdma("sync", lambda e, r=r, c=c, cw=cw, buf=buf, w_in_=w_in_: e.dma_start(
                    out=buf, in_=w_in_[r:r + 128, c:c + cw]), "x%d" % par, wait=w)
                b.cop("vector", lambda e, buf=buf, tmp=tmp: e.tensor_reduce(
                    tmp, buf, AX.X, ALU.add, apply_absolute_value=True),
                    wait=[b.mark("x%d" % par), ("v", b.cnt["v"])], inc="v")
                lm[par] = ("v", b.cnt["v"])
                b.cop("vector", lambda e, acc=acc, tmp=tmp: e.tensor_add(acc, acc, tmp),
                      wait=[("v", b.cnt["v"])], inc="v")
            b.cop("tensor", lambda e, acc=acc: e.matmul(
                pb[6][0:1, 0:1], acc, onescf[:], start=True, stop=True),
                wait=[("v", b.cnt["v"])], inc="t")
            wsv = stats[0:1, 32 + m:33 + m]
            rsv = stats[0:1, 36 + m:37 + m]
            b.cop("vector", lambda e, wsv=wsv, rows=rows, cols=cols: e.tensor_scalar(
                wsv, pb[6][0:1, 0:1], 1.0 / (rows * cols), 1e-5, ALU.mult, ALU.add),
                wait=[("t", b.cnt["t"])], inc="v")
            b.cop("vector", lambda e, wsv=wsv, rsv=rsv: e.reciprocal(rsv, wsv),
                  wait=[("v", b.cnt["v"])], inc="v")
            for (src, dcol) in ((wsv, qslab[:, 83 + m:84 + m]), (rsv, qslab[:, 80 + m:81 + m])):
                b.cop("tensor", lambda e, src=src: e.matmul(
                    pb[6][:, 2:3], ones1f[:], src, start=True, stop=True),
                    wait=[("v", b.cnt["v"])], inc="t")
                b.cop("vector", lambda e, dcol=dcol: e.tensor_copy(dcol, pb[6][:, 2:3]),
                      wait=[("t", b.cnt["t"])], inc="v")
            rs_ready = ("v", b.cnt["v"])
            rcol = qslab[:, 80 + m:81 + m]
            st_m = {0: None, 1: None}
            lm2 = {0: None, 1: None}
            for j, (r, c) in enumerate(chunks):
                par = j % 2
                buf = wbuf[par][:, 0:cw]
                q16 = qs16[par][:, 0:cw]
                w = [lm2[par] or lm[par]]
                b.sdma("sync", lambda e, r=r, c=c, cw=cw, buf=buf, w_in_=w_in_: e.dma_start(
                    out=buf, in_=w_in_[r:r + 128, c:c + cw]), "x%d" % par,
                    wait=[x for x in w if x])
                b.cop("vector", lambda e, buf=buf, cw=cw, rcol=rcol: e.tensor_scalar(
                    m1[:, 0:cw], buf, rcol, 0.5, ALU.mult, ALU.is_ge),
                    wait=[b.mark("x%d" % par), rs_ready], inc="v")
                b.cop("vector", lambda e, buf=buf, cw=cw, rcol=rcol: e.tensor_scalar(
                    m2[:, 0:cw], buf, rcol, -0.5, ALU.mult, ALU.is_le), inc="v")
                lm2[par] = ("v", b.cnt["v"])
                sw = [st_m[par]] if st_m[par] else None
                b.cop("vector", lambda e, q16=q16, cw=cw: e.tensor_sub(
                    q16, m1[:, 0:cw], m2[:, 0:cw]), wait=sw, inc="v")
                b.sdma("sync", lambda e, r=r, c=c, cw=cw, q16=q16, wq_scr_=wq_scr_: e.dma_start(
                    out=wq_scr_[r:r + 128, c:c + cw], in_=q16),
                    "so%d" % (2 + par), wait=[("v", b.cnt["v"])])
                st_m[par] = b.mark("so%d" % (2 + par))

    def p8b_gateup():
        xqfm = G[:, 0:8192]
        wsl = [G[:, 8192:16384], G[:, 16384:24576]]
        gateF = F[:, 0:2048]
        guF = [F[:, 2048 + j * 512:2048 + (j + 1) * 512] for j in range(2)]
        b.cop("vector", lambda e: e.tensor_scalar(
            qslab[:, 16:32], qslab[:, 0:16], qslab[:, 83:84], None, ALU.mult),
            wait=[("v", b.cnt["v"])], inc="v")
        b.cop("vector", lambda e: e.tensor_scalar(
            qslab[:, 32:48], qslab[:, 0:16], qslab[:, 84:85], None, ALU.mult),
            wait=[("v", b.cnt["v"])], inc="v")
        xs_ready = ("v", b.cnt["v"])
        wsl_mm = {0: b.cnt["t"], 1: b.cnt["t"]}
        gu_st = {0: None, 1: None}
        mm_prev = b.cnt["t"]
        gu_evac_v = b.cnt["v"]
        gate_s = b.cnt["s"]
        for tq in range(4):
            for hc in range(NF):
                par = hc % 2
                w = [("t", mm_prev)] if (tq > 0 and hc < 2) else None
                b.sdma("sync", lambda e, tq=tq, hc=hc: e.dma_start_transpose(
                    xqfm[:, hc * 512:(hc + 1) * 512],
                    xq_scr[tq * 512:(tq + 1) * 512, hc * 128:(hc + 1) * 128]),
                    "h1%d" % par, wait=w)
            xql = [b.mark("h10"), b.mark("h11")]
            for is_ in range(16):
                for guw in range(2):
                    par = (is_ * 2 + guw) % 2
                    wq_scr_ = wqg_scr if guw == 0 else wqu_scr
                    for hc in range(NF):
                        w = [("t", wsl_mm[par])] if hc == 0 else None
                        b.sdma("sync", lambda e, is_=is_, hc=hc, par=par, wq_scr_=wq_scr_: e.dma_start_transpose(
                            wsl[par][:, hc * 512:(hc + 1) * 512],
                            wq_scr_[is_ * 512:(is_ + 1) * 512, hc * 128:(hc + 1) * 128]),
                            "wt%d" % par, wait=w)
                    wm = b.mark("wt%d" % par)
                    for hc in range(NF):
                        mmw = None
                        if hc == 0:
                            mmw = xql + [wm]
                            if guw == 0:
                                mmw.append(("v", gu_evac_v))
                            else:
                                mmw.append(("s", gate_s))
                        for tt in range(4):
                            b.cop("tensor", lambda e, hc=hc, tt=tt, par=par: e.matmul(
                                pb[tt][:, :],
                                xqfm[:, hc * 512 + tt * 128:hc * 512 + (tt + 1) * 128],
                                wsl[par][:, hc * 512:(hc + 1) * 512],
                                start=(hc == 0), stop=(hc == NF - 1)),
                                wait=mmw if tt == 0 else None,
                                inc="t" if (hc == NF - 1 and tt == 3) else None)
                    wsl_mm[par] = b.cnt["t"]
                    tv = b.cnt["t"]
                    if guw == 0:
                        for tt in range(4):
                            ttg = tq * 4 + tt
                            sw = [("t", tv), xs_ready]
                            if tt == 0:
                                sw.append(("v", gu_evac_v))
                            b.cop("scalar", lambda e, tt=tt, ttg=ttg: e.activation(
                                gateF[:, tt * 512:(tt + 1) * 512], pb[tt][:, :], AF.Silu,
                                scale=qslab[:, 16 + ttg:17 + ttg]),
                                wait=sw, inc="s")
                        gate_s = b.cnt["s"]
                    else:
                        for tt in range(4):
                            ttg = tq * 4 + tt
                            gpar = tt % 2
                            gw = [("t", tv), ("s", gate_s)]
                            if gu_st[gpar]:
                                gw.append(gu_st[gpar])
                            b.cop("vector", lambda e, tt=tt, ttg=ttg, gpar=gpar: e.scalar_tensor_tensor(
                                guF[gpar], pb[tt][:, :], qslab[:, 32 + ttg:33 + ttg],
                                gateF[:, tt * 512:(tt + 1) * 512], ALU.mult, ALU.mult),
                                wait=gw, inc="v")
                            b.sdma("sync", lambda e, ttg=ttg, is_=is_, gpar=gpar: e.dma_start(
                                out=gu_scr[ttg * 128:(ttg + 1) * 128, is_ * 512:(is_ + 1) * 512],
                                in_=guF[gpar]), "so%d" % gpar, wait=[("v", b.cnt["v"])])
                            gu_st[gpar] = b.mark("so%d" % gpar)
                        gu_evac_v = b.cnt["v"]
            mm_prev = b.cnt["t"]

    def p8c_guquant():
        gub = [[F[:, 0:4096], F[:, 4096:8192]],
               [F[:, 8192:12288], F[:, 12288:16384]]]
        cct = F[:, 16384:20480]
        sgn16 = G[:, 0:8192]
        mk16 = G[:, 8192:12288]
        q16 = [G[:, 12288:16384], G[:, 16384:20480]]
        cons = {0: None, 1: None}
        qst = {0: None, 1: None}
        for tt in range(16):
            par = tt % 2
            for hf in range(2):
                w = [cons[par]] if cons[par] else None
                b.sdma("sync", lambda e, tt=tt, hf=hf, par=par: e.dma_start(
                    out=gub[par][hf],
                    in_=gu_scr[tt * 128:(tt + 1) * 128, hf * 4096:(hf + 1) * 4096]),
                    "x%d" % hf, wait=w)
            lmk = [b.mark("x0"), b.mark("x1")]
            b.cop("vector", lambda e, par=par: e.tensor_reduce(
                stats[:, 40:41], gub[par][0], AX.X, ALU.max, apply_absolute_value=True),
                wait=lmk + [("v", b.cnt["v"])], inc="v")
            b.cop("vector", lambda e, par=par: e.tensor_reduce(
                stats[:, 41:42], gub[par][1], AX.X, ALU.max, apply_absolute_value=True),
                wait=[("v", b.cnt["v"])], inc="v")
            b.cop("vector", lambda e: e.tensor_max(
                stats[:, 42:43], stats[:, 40:41], stats[:, 41:42]),
                wait=[("v", b.cnt["v"])], inc="v")
            b.cop("vector", lambda e, tt=tt: e.tensor_scalar(
                qslab[:, 48 + tt:49 + tt], stats[:, 42:43], 1.0 / 127.0, 1e-5,
                ALU.mult, ALU.add),
                wait=[("v", b.cnt["v"])], inc="v")
            b.cop("vector", lambda e, tt=tt: e.reciprocal(
                stats[:, 43:44], qslab[:, 48 + tt:49 + tt]),
                wait=[("v", b.cnt["v"])], inc="v")
            rs_v = ("v", b.cnt["v"])
            last_v = None
            for hf in range(2):
                gh = gub[par][hf]
                sgh = sgn16[:, hf * 4096:(hf + 1) * 4096]
                sw = [rs_v]
                if last_v:
                    sw.append(last_v)
                b.cop("scalar", lambda e, gh=gh: e.activation(
                    gh, gh, AF.Identity, scale=stats[:, 43:44]), wait=sw, inc="s")
                b.cop("scalar", lambda e, gh=gh, sgh=sgh: e.activation(sgh, gh, AF.Sign),
                      wait=[("s", b.cnt["s"])], inc="s")
                b.cop("vector", lambda e, gh=gh: e.scalar_tensor_tensor(
                    gh, gh, -1.0, gh, ALU.mult, ALU.max),
                    wait=[("s", b.cnt["s"])], inc="v")
                b.cop("vector", lambda e, gh=gh: e.tensor_scalar(mk16[:], gh, 63.5, None, ALU.is_ge))
                b.cop("vector", lambda e: e.tensor_scalar(cct, mk16[:], 64.0, None, ALU.mult))
                for bit in (32, 16, 8, 4, 2, 1):
                    b.cop("vector", lambda e, bb_=float(bit), gh=gh: e.scalar_tensor_tensor(
                        mk16[:], cct, bb_ - 0.5, gh, ALU.add, ALU.is_le))
                    b.cop("vector", lambda e, bb_=float(bit): e.scalar_tensor_tensor(
                        cct, mk16[:], bb_, cct, ALU.mult, ALU.add))
                qw = [qst[hf]] if qst[hf] else None
                b.cop("vector", lambda e, hf=hf, sgh=sgh: e.tensor_mul(q16[hf], cct, sgh),
                      wait=qw, inc="v")
                last_v = ("v", b.cnt["v"])
                b.sdma("sync", lambda e, tt=tt, hf=hf: e.dma_start(
                    out=guq_scr[tt * 128:(tt + 1) * 128, hf * 4096:(hf + 1) * 4096],
                    in_=q16[hf]), "so%d" % hf, wait=[("v", b.cnt["v"])])
                qst[hf] = b.mark("so%d" % hf)
            cons[par] = last_v

    def p8d_gsb():
        gsb_all = F[:, 20480:22528]
        grow = F[0:1, 16384:16896]
        b.cop("vector", lambda e: e.tensor_scalar(
            qslab[:, 64:80], colbank[:, C_GB * 16:C_GB * 16 + 16],
            qslab[:, 85:86], None, ALU.mult), wait=[("v", b.cnt["v"])], inc="v")
        for ss in range(NS):
            tv0 = b.cnt["v"]
            for k in range(4):
                tt = ss * 4 + k
                b.cop("tensor", lambda e, k=k, tt=tt: e.transpose(
                    pb[6][0:1, k * 128:(k + 1) * 128], qslab[:, 48 + tt:49 + tt],
                    identf32[:]),
                    wait=[("v", tv0)] if k == 0 else None,
                    inc="t" if k == 3 else None)
            b.cop("vector", lambda e: e.tensor_copy(grow, pb[6][0:1, 0:512]),
                  wait=[("t", b.cnt["t"])], inc="v")
            b.cop("tensor", lambda e: e.matmul(
                pb[5][:, :], ones1f[:], grow, start=True, stop=True),
                wait=[("v", b.cnt["v"])], inc="t")
            b.cop("vector", lambda e, ss=ss: e.tensor_copy(
                gsb_all[:, ss * 512:(ss + 1) * 512], pb[5][:, :]),
                wait=[("t", b.cnt["t"])], inc="v")

    def p8e_down():
        gsb_all = F[:, 20480:22528]
        guqfm = G[:, 0:32768]
        wTb8 = [G[:, 32768:33536], G[:, 33536:34304]]
        outb2 = [G[:, 34304 + j * 512:34304 + (j + 1) * 512] for j in range(2)]
        GRP8 = ((0, 6), (6, 6), (12, 4))
        smm = b.cnt["t"]
        wT_mm = {0: b.cnt["t"], 1: b.cnt["t"]}
        b2_st = {0: None, 1: None}
        for ss in range(NS):
            for ic in range(64):
                par = ic % 2
                w = [("t", smm)] if (ss > 0 and ic < 2) else None
                b.sdma("sync", lambda e, ss=ss, ic=ic: e.dma_start_transpose(
                    guqfm[:, ic * 512:(ic + 1) * 512],
                    guq_scr[ss * 512:(ss + 1) * 512, ic * 128:(ic + 1) * 128]),
                    "h1%d" % par, wait=w)
            gql = [b.mark("h10"), b.mark("h11")]
            evac_v = b.cnt["v"]
            for (fst, ncol) in GRP8:
                for ic in range(64):
                    par = ic % 2
                    buf = wTb8[par]
                    b.sdma("sync", lambda e, fst=fst, ncol=ncol, ic=ic, buf=buf: e.dma_start_transpose(
                        buf[:, 0:ncol * 128],
                        wqd_scr[fst * 128:fst * 128 + ncol * 128, ic * 128:(ic + 1) * 128]),
                        "wt%d" % par, wait=[("t", wT_mm[par])])
                    mmw = [b.mark("wt%d" % par)]
                    if ic == 0:
                        mmw += gql + [("v", evac_v)]
                    for fc6 in range(ncol):
                        b.cop("tensor", lambda e, fc6=fc6, ic=ic, buf=buf: e.matmul(
                            pb[fc6][:, :], buf[:, fc6 * 128:(fc6 + 1) * 128],
                            guqfm[:, ic * 512:(ic + 1) * 512],
                            start=(ic == 0), stop=(ic == 63)),
                            wait=mmw if fc6 == 0 else None,
                            inc="t" if fc6 == ncol - 1 else None)
                    wT_mm[par] = b.cnt["t"]
                tv = b.cnt["t"]
                for fc6 in range(ncol):
                    fc = fst + fc6
                    opar = fc6 % 2
                    ew = [("t", tv)] if fc6 == 0 else []
                    if b2_st[opar]:
                        ew.append(b2_st[opar])
                    b.cop("vector", lambda e, fc=fc, fc6=fc6, ss=ss, opar=opar: e.scalar_tensor_tensor(
                        outb2[opar], pb[fc6][:, :], qslab[:, 64 + fc:65 + fc],
                        gsb_all[:, ss * 512:(ss + 1) * 512], ALU.mult, ALU.mult),
                        wait=ew or None, inc="v")
                    b.sdma("sync", lambda e, fc=fc, ss=ss, opar=opar: e.dma_start(
                        out=branch2_scr[fc * 128:(fc + 1) * 128, ss * 512:(ss + 1) * 512],
                        in_=outb2[opar]), "so%d" % opar, wait=[("v", b.cnt["v"])])
                    b2_st[opar] = b.mark("so%d" % opar)
                evac_v = b.cnt["v"]
            smm = b.cnt["t"]

    def p9_final():
        xt9 = [F[:, 0:H], F[:, H:2 * H]]
        ot9 = [F[:, 2 * H:3 * H], F[:, 3 * H:4 * H]]
        btm9 = [G[:, 0:H], G[:, H:2 * H]]
        vm = {0: None, 1: None}
        stm = {0: None, 1: None}
        for i in range(NT):
            par = i % 2
            w = [vm[par]] if vm[par] else None
            b.sdma("sync", lambda e, i=i, par=par: e.dma_start(
                out=xt9[par], in_=x2_scr[i * 128:(i + 1) * 128, :]), "x%d" % par, wait=w)
            b.sdma("sync", lambda e, i=i, par=par: e.dma_start_transpose(
                btm9[par], branch2_scr[:, i * 128:(i + 1) * 128]), "h1%d" % par, wait=w)
            aw = [b.mark("x%d" % par), b.mark("h1%d" % par)]
            if stm[par]:
                aw.append(stm[par])
            b.cop("vector", lambda e, par=par: e.tensor_add(ot9[par], xt9[par], btm9[par]),
                  wait=aw, inc="v")
            vm[par] = ("v", b.cnt["v"])
            b.sdma("sync", lambda e, i=i, par=par: e.dma_start(
                out=out[i * 128:(i + 1) * 128, :], in_=ot9[par]), "st",
                wait=[("v", b.cnt["v"])])
            stm[par] = b.mark("st")

    if "sm" in dbg_names:
        dbg_outs["dbg_sm"] = b.dram("dbg_sm", [128, NT * 8], F32, kind="ExternalOutput")
    if "xt2" in dbg_names:
        dbg_outs["dbg_xt2"] = b.dram("dbg_xt2", [128, H], F32, kind="ExternalOutput")

    import os as _os
    n_delay = int(_os.environ.get("START_DELAY", "0"))
    if n_delay:
        for j in range(n_delay):
            b.cop("gpsimd", lambda e: e.memset(G[:, 0:2048], 0.0),
                  inc="g" if j == n_delay - 1 else None)
        b.barrier()
    p0_tables()
    p0_mod()
    p0_qk_rows()
    p0_cols()
    b.barrier()
    p1_ln1()
    b.barrier()
    if lvl >= 1:
        p2_qkv()
        b.barrier()
    if lvl >= 6:
        p3_rope()
        b.barrier()
    if lvl >= 7:
        p4_ttt()
        b.barrier()
    if lvl >= 8:
        p5_attn()
        b.barrier()
    if lvl >= 9:
        p6_oproj()
        b.barrier()
    if lvl >= 10:
        p7_ln2()
        b.barrier()
    if lvl >= 11:
        p8a_wquant()
        b.barrier()
        p8b_gateup()
        b.barrier()
        p8c_guquant()
        b.barrier()
        p8d_gsb()
        b.barrier()
        p8e_down()
        b.barrier()

    def dump_dram(name, src_ap, shape):
        o = b.dram(name, shape, F32, kind="ExternalOutput")
        n = shape[0]
        for r0 in range(0, n, 128):
            rr = min(128, n - r0)
            stage = G[:rr, 0:shape[1]]
            stf = F[:rr, 0:shape[1]]
            b.barrier()
            b.sdma("sync", lambda e, stage=stage, r0=r0, rr=rr: e.dma_start(
                out=stage, in_=src_ap[r0:r0 + rr, :]), "ld")
            b.cop("vector", lambda e, stage=stage, stf=stf: e.tensor_copy(stf, stage),
                  wait=[b.mark("ld")], inc="v")
            b.sdma("sync", lambda e, stf=stf, r0=r0, rr=rr, o=o: e.dma_start(
                out=o[r0:r0 + rr, :], in_=stf), "st", wait=[("v", b.cnt["v"])])
        b.barrier()
        dbg_outs[name] = o

    def dump_dram_f32(name, src_ap, shape):
        o = b.dram(name, shape, F32, kind="ExternalOutput")
        for r0 in range(0, shape[0], 128):
            stf = F[:, 0:shape[1]]
            b.barrier()
            b.sdma("sync", lambda e, stf=stf, r0=r0: e.dma_start(
                out=stf, in_=src_ap[r0:r0 + 128, :]), "ld")
            b.sdma("sync", lambda e, stf=stf, r0=r0, o=o: e.dma_start(
                out=o[r0:r0 + 128, :], in_=stf), "st", wait=[b.mark("ld")])
        b.barrier()
        dbg_outs[name] = o

    for nm, src, shp in (("dbg_h", h_scr, [T, H]),
                         ("dbg_ottt", ottt_scr, [T, H]),
                         ("dbg_attn", attn_scr, [H, T]),
                         ("dbg_qa", qa_scr, [T, H]), ("dbg_ka", ka_scr, [T, H]),
                         ("dbg_fq", fq_scr, [H, T]),
                         ("dbg_fk", fk_scr, [H, T]), ("dbg_fv", fv_scr, [H, T]),
                         ("dbg_qn", qn_scr, [H, T]), ("dbg_kn", kn_scr, [H, T]),
                         ("dbg_v", v_scr, [H, T]),
                         ("dbg_branch", branch_scr, [H, T]),
                         ("dbg_xq", xq_scr, [T, H]),
                         ("dbg_wqg", wqg_scr, [I, H]),
                         ("dbg_guq", guq_scr, [T, I]),
                         ("dbg_b2", branch2_scr, [H, T])):
        if nm in dbg_names:
            dump_dram(nm, src, shp)
    if "dbg_x2" in dbg_names:
        dump_dram_f32("dbg_x2", x2_scr, [T, H])
    if "dbg_gu" in dbg_names:
        dump_dram_f32("dbg_gu", gu_scr, [T, I])
    if "dbg_qs" in dbg_names:
        o = b.dram("dbg_qs", [128, 96], F32, kind="ExternalOutput")
        b.sdma("sync", lambda e, o=o: e.dma_start(out=o[:, :], in_=qslab[:]),
               "st", wait=[("v", b.cnt["v"])])
        dbg_outs["dbg_qs"] = o
    if "rows" in dbg_names:
        o = b.dram("dbg_rows", [NROWS, H], F32, kind="ExternalOutput")
        b.sdma("sync", lambda e, o=o: e.dma_start(out=o[0:NROWS, :], in_=rowbank[0:NROWS, :]),
               "st", wait=[b.mark("rb")])
        dbg_outs["dbg_rows"] = o
    if "x_end" in dbg_names:
        o = b.dram("dbg_x", [T, H], F32, kind="ExternalOutput")
        for r0 in range(0, T, 128):
            stf = F[:, 0:H]
            b.barrier()
            b.sdma("sync", lambda e, r0=r0, stf=stf: e.dma_start(out=stf, in_=x_in[r0:r0 + 128, :]), "ld")
            b.sdma("sync", lambda e, r0=r0, stf=stf, o=o: e.dma_start(out=o[r0:r0 + 128, :], in_=stf),
                   "st", wait=[b.mark("ld")])
        b.barrier()
        dbg_outs["dbg_x"] = o
    if "lr" in dbg_names:
        o = b.dram("dbg_lr", [128, NT * 12], F32, kind="ExternalOutput")
        b.sdma("sync", lambda e, o=o: e.dma_start(out=o[:, :], in_=l_slab[:]),
               "st", wait=[("v", b.cnt["v"])])
        dbg_outs["dbg_lr"] = o

    b.barrier()
    if lvl >= 12:
        p9_final()
    else:
        zt = F[:, 0:H]
        b.cop("vector", lambda e: e.memset(zt, 0.0), inc="v")
        vz = b.cnt["v"]
        for i in range(NT):
            b.sdma("sync", lambda e, i=i: e.dma_start(out=out[i * 128:(i + 1) * 128, :], in_=zt),
                   "st", wait=[("v", vz)])
    b.cop("sync", lambda e: e.nop(), wait=[("st", b.cnt["st"])])
    return b, dbg_outs


def shard_inputs(inputs, core):
    d = inputs
    return {
        "x": np.ascontiguousarray(d["L0_attn_hidden_states"][core]),
        "cvec": np.ascontiguousarray(d["L0_c"][core:core + 1]),
        "wmod": d["L0_wmod"],
        "qkv_w": d["L0_attn_qkv_w"],
        "q_norm_w": d["L0_attn_q_norm_w"].reshape(1, H),
        "k_norm_w": d["L0_attn_k_norm_w"].reshape(1, H),
        "qk_scale": np.ascontiguousarray(d["L0_attn_qk_scale"].reshape(1, 2 * H)),
        "qk_offset": np.ascontiguousarray(d["L0_attn_qk_offset"].reshape(1, 2 * H)),
        "lr_w": d["L0_attn_lr_w"],
        "lr_b": d["L0_attn_lr_b"].reshape(1, 12),
        "w0": d["L0_attn_w0"].reshape(NFW * FD, FD),
        "w1": d["L0_attn_w1"].reshape(NFW * FD, FD),
        "w2": d["L0_attn_w2"].reshape(NFW * FD, FD),
        "ttt_norm_w": d["L0_attn_ttt_norm_w"].reshape(1, FD),
        "o_proj_w": d["L0_attn_o_proj_w"],
        "w_gate": d["L0_ffn_w_gate"],
        "w_up": d["L0_ffn_w_up"],
        "w_down": d["L0_ffn_w_down"],
    }


def kernel(**inputs):
    b, _ = build()
    nc = b.finish()
    in_maps = [shard_inputs(inputs, c) for c in range(8)]
    res = run_bass_kernel_spmd(nc, in_maps, core_ids=list(range(8)))
    return np.stack([np.asarray(r["out"], np.float32) for r in res.results])
